# version 03 (final)
# baseline (speedup 1.0000x reference)
import numpy as np
import os as _os
import concourse.bass as bass
import concourse.mybir as mybir
from concourse.bass_isa import ReduceOp
from concourse.bass_utils import run_bass_kernel_spmd
from contextlib import ExitStack

F32 = mybir.dt.float32
BF16 = mybir.dt.bfloat16
AF = mybir.ActivationFunctionType
ALU = mybir.AluOpType
AX = mybir.AxisListType

N_CORES = 8
B, L, D = 4, 4096, 1024
LL = L // 2
DC = D // 128
HID = 4096
HC = HID // 128
MAGIC = 12582912.0
LT = 512
NLT = LL // LT

SUBLAYERS = (1, 2, 3, 4)
USE_REMOTE = False

ENGINES = ("pe", "act", "dve", "pool", "sp")


SELF_SER = {"dve": "dve_c", "act": "act_c", "pool": "pool_c"}


class Sched:
    def __init__(self):
        self.items = {e: [] for e in ENGINES}
        self.counts = {}
        self.pending = {e: [] for e in ENGINES}

    def op(self, eng, fn, isdma=False):
        it = {"fn": fn, "waits": list(self.pending[eng]), "incs": [], "dma": isdma}
        self.pending[eng] = []
        self.items[eng].append(it)
        if not isdma and eng in SELF_SER:
            sem = SELF_SER[eng]
            prev = self.counts.get(sem, 0)
            it["selfwait"] = (sem, prev)
            self.counts[sem] = prev + 1
            it["incs"].append((sem, 1))
        return it

    def sig(self, eng, sem, amt=1):
        it = self.items[eng][-1]
        if not it["dma"] and eng in SELF_SER and sem == SELF_SER[eng]:
            return (sem, self.counts[sem])
        self.counts[sem] = self.counts.get(sem, 0) + amt
        it["incs"].append((sem, amt))
        return (sem, self.counts[sem])

    def bump(self, sem, amt):
        self.counts[sem] = self.counts.get(sem, 0) + amt
        return (sem, self.counts[sem])

    def aft(self, eng, *evs):
        for ev in evs:
            if ev is not None:
                self.pending[eng].append(ev)

    def mark(self, sem):
        return (sem, self.counts.get(sem, 0))

    def replay(self, name, eng, sems):
        for it in self.items[name]:
            waits = {}
            for sem, val in it["waits"]:
                waits[sem] = max(waits.get(sem, 0), val)
            for sem, val in sorted(waits.items()):
                if val > 0:
                    eng.wait_ge(sems[sem], val)
            inst = it["fn"](eng) if it["fn"] is not None else None
            if inst is not None and it.get("selfwait") is not None:
                sem, val = it["selfwait"]
                if val > 0:
                    inst._wait_ge(sems[sem], val)
            for i, (sem, amt) in enumerate(it["incs"]):
                if inst is not None and i == 0:
                    inst.then_inc(sems[sem], amt)
                else:
                    eng.sem_inc(sems[sem], amt)


def build_nc(sublayers):
    nc = bass.Bass("TRN2", target_bir_lowering=False, debug=False)

    def din(name, shape, dtype=F32):
        return nc.dram_tensor(name, list(shape), dtype, kind="ExternalInput").ap()

    x_in = din("x_fm", (D, LL))
    w1t = din("w1t", (D, HID))
    w2t = din("w2t", (HID, D))
    ffn_cw = din("ffn_cw", (3, HID))
    ffn_cb = din("ffn_cb", (1, HID))
    maskPN = din("maskPN", (1, 16))
    onesrow_in = din("onesrow", (1, 128))
    onescol_in = din("onescol", (128, 1))
    inpw_x = din("inpw_x", (D, 1024))
    inpw_z = din("inpw_z", (D, 1024))
    xpw = din("xpw", (1024, 160))
    dtpw = din("dtpw", (128, 1024))
    opw = din("opw", (2048, D))
    alog = din("alog", (128, 128))
    mcw = din("mcw", (128, 32))
    mcb = din("mcb", (128, 8))
    mdtb = din("mdtb", (128, 8))
    mdc = din("mdc", (128, 8))
    selnf = din("selnf", (16, 2048))
    halfsel = din("halfsel", (1, 8))
    ident = din("ident", (128, 128), BF16)
    wf_in = din("wf", (32, 128, 8448), BF16)
    iw_in = din("iw", (4, 66, 128, 1024), BF16)
    identf_in = din("identf", (128, 128))
    agidx1_in = din("agidx1", (128, 128), mybir.dt.int32)
    agidx2_in = din("agidx2", (128, 32), mybir.dt.int32)
    care_t = din("care_t", (32, 128))
    cimt = din("cimt", (32, 128))
    lare_t = din("lare_t", (32, 128))
    aimt = din("aimt", (32, 128))
    ldt_r = din("ldt_r", (1, 128))
    dtile_in = din("dtile", (1, 512))
    wat = din("wat", (D, 2048))
    wab = din("wab", (128, 16))
    lnones_in = din("lnones", (128, 3), BF16)

    y_out = nc.dram_tensor("y_fm", [D, LL], F32, kind="ExternalOutput").ap()

    w1q_hbm = nc.dram_tensor("w1q", [DC, 128, HC * 128], BF16, kind="Internal").ap()
    w2q_hbm = nc.dram_tensor("w2q", [HC, 128, DC * 128], BF16, kind="Internal").ap()
    sact_hbm = nc.dram_tensor("sact_sp", [HC, 128, LL], F32, kind="Internal").ap()
    xq2_hbm = nc.dram_tensor("xq2_sp", [HC, 128, LL], BF16, kind="Internal").ap()

    s = Sched()
    ctx = ExitStack()
    semd = {}

    _nalloc = [0]

    def sb(shape, dtype=F32):
        _nalloc[0] += 1
        return ctx.enter_context(nc.sbuf_tensor(f"sb{_nalloc[0]}", list(shape), dtype))

    def psum(shape, dtype=F32):
        _nalloc[0] += 1
        return ctx.enter_context(nc.psum_tensor(f"ps{_nalloc[0]}", list(shape), dtype))

    resid = [sb([128, LL]) for _ in range(DC)]
    xq_ext = [sb([128, LL + 2], BF16) for _ in range(DC)]
    R0 = sb([128, LL]); R1 = sb([128, LL]); R2 = sb([128, LL])
    R3 = sb([128, LL]); R4 = sb([128, LL])
    r_mean, r_rstd, r_a, r_b, r_t = R0, R1, R2, R3, R4
    r_bsc, r_q2, r_b2 = R4, R1, R2
    h_t = sb([128, LT])
    prev2 = [sb([128, 2]) for _ in range(HC)]
    halocols = [sb([128, 2]) for _ in range(HC)]
    convtmp = sb([128, LT])
    sacttmp = [sb([128, LT]) for _ in range(2)]
    sback = [sb([128, LL // 2]) for _ in range(2)]
    xq2r = [sb([128, LL // 2], BF16) for _ in range(2)]
    wk1 = [sb([128, 4096], BF16) for _ in range(2)]
    wk2 = [sb([128, 512], BF16) for _ in range(2)]
    xr2 = [sb([128, LT], BF16) for _ in range(2)]
    wstream = [sb([128, 2048]) for _ in range(2)]
    cw0 = sb([128, HC]); cw1 = sb([128, HC]); cw2 = sb([128, HC]); cbias = sb([128, HC])
    swinv = sb([128, 2]); swmul = sb([128, 2])
    maskp = sb([128, 8]); maskn = sb([128, 8])
    sendf = sb([128, 18])
    bshalo = sb([128, 2])
    cc_bufs = []
    thin = sb([128, 16])
    minis = sb([128, HC])
    onesrow = sb([1, 128])
    onescol = sb([128, 1])
    lnones = sb([128, 3], BF16)
    castbuf = sb([128, LL], BF16)
    wqsb = castbuf
    r_sc = wstream[0]

    p_mm = [psum([128, LT]) for _ in range(4)]
    p_big = psum([128, 2048])
    p_h = p_big

    def dma(queue, out_ap, in_ap, sem):
        s.op(queue, lambda eng, o=out_ap, i=in_ap: eng.dma_start(out=o, in_=i), isdma=True)
        return s.sig(queue, sem, 16)

    x_t = x_in.rearrange("(c p) l -> c p l", p=128)
    ev_x = None
    for c in range(DC):
        ev_x = dma("sp", resid[c][:, :], x_t[c], "ld_x")

    cw_h = ffn_cw.rearrange("t (h p) -> t h p", p=128)
    for j, buf in enumerate((cw0, cw1, cw2)):
        for h in range(HC):
            dma("pool", buf[:, h:h + 1], cw_h[j, h].rearrange("(p o) -> p o", o=1), "ld_const")
    cb_h = ffn_cb.rearrange("o (h p) -> o h p", p=128)
    for h in range(HC):
        dma("pool", cbias[:, h:h + 1], cb_h[0, h].rearrange("(p o) -> p o", o=1), "ld_const")
    dma("pool", maskp[0:1, :], maskPN[:, 0:8], "ld_const")
    dma("pool", maskn[0:1, :], maskPN[:, 8:16], "ld_const")
    dma("pool", onesrow[0:1, :], onesrow_in, "ld_const")
    dma("pool", onescol[:, :], onescol_in, "ld_const")
    dma("pool", lnones[:, :], lnones_in, "ld_const")
    ev_small = s.mark("ld_const")

    def preduce(h, c0, width, op, ev_ready=None, sc=None, sc0=0):
        if sc is None:
            sc = r_sc
        s.aft("pool", ev_ready, s.mark("dve_c"))
        s.aft("dve", ev_ready)
        evd = None
        k = 64
        while k >= 1:
            ev = dma("pool", sc[0:k, sc0:sc0 + width], h[k:2 * k, c0:c0 + width], "ld_pr")
            s.aft("dve", ev)
            s.op("dve", lambda eng, k=k, h=h, c0=c0, width=width, op=op, sc=sc, sc0=sc0: eng.tensor_tensor(
                h[0:k, c0:c0 + width], h[0:k, c0:c0 + width], sc[0:k, sc0:sc0 + width], op))
            evd = s.sig("dve", "dve_c")
            s.aft("pool", evd)
            k //= 2
        return evd

    def pbcast(sh, sc0, dh, dc0, width, bank, *evs):
        for nt in range((width + 511) // 512):
            s.aft("pe", *evs)
            w = min(512, width - nt * 512)
            s.op("pe", lambda eng, nt=nt, w=w, sh=sh, sc0=sc0, bank=bank: eng.matmul(
                p_big[:, ((bank + nt) % 4) * 512:((bank + nt) % 4) * 512 + w],
                onesrow[0:1, :],
                sh[0:1, sc0 + nt * 512:sc0 + nt * 512 + w],
                start=True, stop=True))
        ev_mm = s.sig("pe", "pe_c")
        for nt in range((width + 511) // 512):
            w = min(512, width - nt * 512)
            s.aft("dve", ev_mm)
            s.op("dve", lambda eng, nt=nt, w=w, dh=dh, dc0=dc0, bank=bank: eng.tensor_copy(
                dh[:, dc0 + nt * 512:dc0 + nt * 512 + w],
                p_big[:, ((bank + nt) % 4) * 512:((bank + nt) % 4) * 512 + w]))
        ev = s.sig("dve", "dve_c")
        return ev

    ev_maskbc = pbcast(maskp, 0, maskp, 0, 8, 0, ev_small)
    ev_masks = pbcast(maskn, 0, maskn, 0, 8, 1, ev_small, ev_maskbc)

    def quantize_weight(w_ap, n_k, n_m, wq_ap, slot):
        w_tl = w_ap.rearrange("(c p) (h m) -> c h p m", p=128, h=2)
        M = n_m * 128 // 2
        n_ch = n_k * 2
        ld = []
        for c in range(n_ch):
            if c >= 2:
                s.aft("sp", ld[c - 2][1])
            ev = dma("sp", wstream[c % 2][:, :M], w_tl[c // 2, c % 2], "q_e" if c % 2 == 0 else "q_o")
            ld.append([ev, None])
            s.aft("dve", ev)
            s.op("dve", lambda eng, c=c: eng.tensor_reduce(
                thin[:, 4 + c % 2:5 + c % 2], wstream[c % 2][:, :M], AX.X, ALU.add,
                apply_absolute_value=True))
            if c == 0:
                s.op("dve", lambda eng: eng.tensor_copy(thin[:, 2:3], thin[:, 4:5]))
            else:
                s.op("dve", lambda eng, c=c: eng.tensor_add(
                    thin[:, 2:3], thin[:, 2:3], thin[:, 4 + c % 2:5 + c % 2]))
            ld[c][1] = s.sig("dve", "dve_c")
        ev_acc = ld[n_ch - 1][1]
        ev_red = preduce(thin, 2, 1, ALU.add, ev_acc, sc=thin, sc0=12)
        s.aft("dve", ev_red)
        s.op("dve", lambda eng: eng.tensor_scalar(
            thin[0:1, 8:9], thin[0:1, 2:3], 1.0 / (128 * n_ch * M), 1e-5,
            ALU.mult, ALU.max))
        s.op("dve", lambda eng: eng.reciprocal(thin[0:1, 9:10], thin[0:1, 8:9]))
        ev_sc = s.sig("dve", "dve_c")
        ev_sa_ = pbcast(thin, 8, swinv, slot, 1, 2, ev_sc)
        ev_sw = pbcast(thin, 9, swmul, slot, 1, 3, ev_sc, ev_sa_)
        ev_store = None
        for c in range(n_ch):
            s.aft("sp", s.mark("dve_c"))
            ev = dma("sp", wstream[c % 2][:, :M], w_tl[c // 2, c % 2], "q_e" if c % 2 == 0 else "q_o")
            s.aft("dve", ev, ev_sw, ev_store)
            if c == 0:
                s.aft("dve", s.mark("st_wq1"), s.mark("st_wq2"))
            s.op("dve", lambda eng, c=c, M=M: eng.tensor_scalar(
                wstream[c % 2][:, :M], wstream[c % 2][:, :M], swmul[:, slot:slot + 1],
                MAGIC, ALU.mult, ALU.add))
            s.op("dve", lambda eng, c=c, M=M: eng.tensor_scalar(
                wstream[c % 2][:, :M], wstream[c % 2][:, :M], MAGIC, 1.0,
                ALU.subtract, ALU.min))
            s.op("dve", lambda eng, c=c, M=M: eng.tensor_scalar(
                wqsb[:, :M], wstream[c % 2][:, :M], -1.0, None, ALU.max))
            ev_q = s.sig("dve", "dve_c")
            s.aft("sp", ev_q)
            ev_store = dma("sp", wq_ap[c // 2][:, (c % 2) * M:(c % 2 + 1) * M],
                           wqsb[:, :M], "st_wq1" if wq_ap is w1q_hbm else "st_wq2")
        return ev_store

    if 2 in sublayers or 4 in sublayers:
        ev_w1q = quantize_weight(w1t, DC, HC, w1q_hbm, 0)
        ev_w2q = quantize_weight(w2t, HC, DC, w2q_hbm, 1)

    def emit_ffn_prep(ev_ready):
        s.aft("act", s.mark("st_wq1"), s.mark("st_wq2"), s.mark("m_lnst"))
        ev_mm = None
        for c in range(DC):
            s.aft("act", ev_ready)
            if ev_mm is not None:
                s.aft("act", ev_mm)
            s.op("act", lambda eng, c=c: eng.activation(castbuf[:, :], resid[c][:, :], AF.Square))
            ev_cs = s.sig("act", "act_c")
            for nt in range(NLT):
                s.aft("pe", ev_cs)
                if c == 0:
                    s.aft("pe", ev_small, ev_ready, ev_masks)
                s.op("pe", lambda eng, c=c, nt=nt: eng.matmul(
                    p_big[0:1, nt * 512:(nt + 1) * 512], onescol[:, :],
                    resid[c][:, nt * 512:(nt + 1) * 512],
                    start=(c == 0), stop=(c == DC - 1)))
                s.op("pe", lambda eng, c=c, nt=nt: eng.matmul(
                    p_mm[nt][0:1, :], lnones[:, 0:1],
                    castbuf[:, nt * 512:(nt + 1) * 512],
                    start=(c == 0), stop=(c == DC - 1)))
            ev_mm = s.sig("pe", "pe_c")
        s.aft("dve", ev_mm)
        s.op("dve", lambda eng: eng.tensor_scalar(
            r_mean[0:1, :], p_big[0:1, :], 1.0 / D, None, ALU.mult))
        for nt in range(NLT):
            s.op("dve", lambda eng, nt=nt: eng.tensor_scalar(
                r_rstd[0:1, nt * 512:(nt + 1) * 512], p_mm[nt][0:1, :], 1.0 / D, None, ALU.mult))
        s.op("dve", lambda eng: eng.tensor_mul(r_b[0:1, :], r_mean[0:1, :], r_mean[0:1, :]))
        s.op("dve", lambda eng: eng.tensor_sub(r_rstd[0:1, :], r_rstd[0:1, :], r_b[0:1, :]))
        s.op("dve", lambda eng: eng.tensor_scalar(r_rstd[0:1, :], r_rstd[0:1, :], 1e-5, None, ALU.add))
        s.op("dve", lambda eng: eng.reciprocal(r_rstd[0:1, :], r_rstd[0:1, :]))
        ev_v = s.sig("dve", "dve_c")
        s.aft("act", ev_v)
        s.op("act", lambda eng: eng.activation(r_rstd[0:1, :], r_rstd[0:1, :], AF.Sqrt))
        ev_r = s.sig("act", "act_c")
        s.aft("dve", ev_ready)
        s.op("dve", lambda eng: eng.tensor_copy(r_a[:, :], resid[0][:, :]))
        for c in range(1, DC):
            s.op("dve", lambda eng, c=c: eng.tensor_tensor(r_a[:, :], r_a[:, :], resid[c][:, :], ALU.max))
        ev_t1 = s.sig("dve", "dve_c")
        ev_rmax = preduce(r_a, 0, LL, ALU.max, ev_t1)
        s.op("dve", lambda eng: eng.tensor_scalar(r_b[:, :], resid[0][:, :], -1.0, None, ALU.mult))
        for c in range(1, DC):
            s.op("dve", lambda eng, c=c: eng.scalar_tensor_tensor(
                r_b[:, :], resid[c][:, :], -1.0, r_b[:, :], ALU.mult, ALU.max))
        ev_t2 = s.sig("dve", "dve_c")
        ev_rmin = preduce(r_b, 0, LL, ALU.max, ev_t2)
        s.aft("dve", ev_rmax, ev_rmin, ev_r)
        s.op("dve", lambda eng: eng.tensor_sub(r_a[0:1, :], r_a[0:1, :], r_mean[0:1, :]))
        s.op("dve", lambda eng: eng.tensor_add(r_b[0:1, :], r_b[0:1, :], r_mean[0:1, :]))
        s.op("dve", lambda eng: eng.tensor_tensor(r_a[0:1, :], r_a[0:1, :], r_b[0:1, :], ALU.max))
        s.op("dve", lambda eng: eng.tensor_mul(r_a[0:1, :], r_a[0:1, :], r_rstd[0:1, :]))
        s.op("dve", lambda eng: eng.tensor_scalar(r_a[0:1, :], r_a[0:1, :], 1e-5, None, ALU.max))
        s.op("dve", lambda eng: eng.tensor_scalar(
            r_bsc[0:1, :], r_a[0:1, :], swinv[0:1, 0:1], 1.0 / 127.0, ALU.mult, ALU.mult))
        s.op("dve", lambda eng: eng.reciprocal(r_a[0:1, :], r_a[0:1, :]))
        s.op("dve", lambda eng: eng.tensor_scalar(r_a[0:1, :], r_a[0:1, :], 127.0, None, ALU.mult))
        s.op("dve", lambda eng: eng.tensor_mul(r_a[0:1, :], r_a[0:1, :], r_rstd[0:1, :]))
        ev_rows0 = s.sig("dve", "dve_c")
        ev_b1 = pbcast(r_mean, 0, r_mean, 0, LL, 0, ev_rows0)
        ev_b2 = pbcast(r_a, 0, r_a, 0, LL, 1, ev_rows0, ev_b1)
        ev_rows = pbcast(r_bsc, 0, r_bsc, 0, LL, 2, ev_rows0, ev_b2)
        for c in range(DC):
            s.aft("dve", ev_rows)
            s.op("dve", lambda eng, c=c: eng.tensor_sub(r_b[:, :], resid[c][:, :], r_mean[:, :]))
            s.op("dve", lambda eng, c=c: eng.tensor_mul(r_b[:, :], r_b[:, :], r_a[:, :]))
            s.op("dve", lambda eng, c=c: eng.tensor_scalar(
                xq_ext[c][:, 2:2 + LL], r_b[:, :], MAGIC, MAGIC, ALU.add, ALU.subtract))
        ev_xq = s.sig("dve", "dve_c")

        q = len(cc_bufs)
        cc_in = nc.dram_tensor(f"cch_in{q}", [128, 18], F32, kind="Internal")
        cc_out = nc.dram_tensor(f"cch_out{q}", [8, 128, 18], F32, kind="Internal",
                                addr_space="Shared")
        cc_bufs.append((cc_in, cc_out))
        rxa = sb([128, 8 * 18])
        s.aft("dve", s.mark("cc_st"))
        for c in range(DC):
            s.op("dve", lambda eng, c=c: eng.tensor_copy(sendf[:, c:c + 1], xq_ext[c][:, 2:3]))
            s.op("dve", lambda eng, c=c: eng.tensor_copy(sendf[:, 8 + c:9 + c], xq_ext[c][:, 1 + LL:2 + LL]))
        s.op("dve", lambda eng: eng.tensor_copy(sendf[:, 16:17], r_bsc[:, 0:1]))
        s.op("dve", lambda eng: eng.tensor_copy(sendf[:, 17:18], r_bsc[:, LL - 1:LL]))
        ev_send = s.sig("dve", "dve_c")
        s.aft("pool", ev_send)
        ev_st = dma("pool", cc_in.ap(), sendf[:, :], "cc_st")
        s.aft("pool", ev_st)
        s.op("pool", lambda eng, cc_in=cc_in, cc_out=cc_out: eng.collective_compute(
            "AllGather", ALU.bypass, ins=[cc_in.ap()], outs=[cc_out.ap()],
            replica_groups=[list(range(8))]), isdma=True)
        ev_cc = s.sig("pool", "ccs")
        s.aft("pool", ev_cc)
        for j in range(8):
            ev_rx = dma("pool", rxa[:, 18 * j:18 * (j + 1)], cc_out.ap()[j], "cc_ld")
        for c in range(DC):
            s.aft("dve", ev_rx, ev_masks)
            s.op("dve", lambda eng, c=c: eng.tensor_scalar(
                xq_ext[c][:, 0:1], rxa[:, 8 + c:9 + c], maskp[:, 0:1], None, ALU.mult))
            for j in range(1, 8):
                s.op("dve", lambda eng, c=c, j=j: eng.scalar_tensor_tensor(
                    xq_ext[c][:, 0:1], rxa[:, j * 18 + 8 + c:j * 18 + 9 + c],
                    maskp[:, j:j + 1], xq_ext[c][:, 0:1], ALU.mult, ALU.add))
            s.op("dve", lambda eng, c=c: eng.tensor_scalar(
                xq_ext[c][:, 1:2], rxa[:, c:c + 1], maskn[:, 0:1], None, ALU.mult))
            for j in range(1, 8):
                s.op("dve", lambda eng, c=c, j=j: eng.scalar_tensor_tensor(
                    xq_ext[c][:, 1:2], rxa[:, j * 18 + c:j * 18 + c + 1],
                    maskn[:, j:j + 1], xq_ext[c][:, 1:2], ALU.mult, ALU.add))
        s.aft("dve", ev_rx, ev_masks)
        s.op("dve", lambda eng: eng.tensor_scalar(
            bshalo[:, 0:1], rxa[:, 17:18], maskp[:, 0:1], None, ALU.mult))
        for j in range(1, 8):
            s.op("dve", lambda eng, j=j: eng.scalar_tensor_tensor(
                bshalo[:, 0:1], rxa[:, j * 18 + 17:j * 18 + 18], maskp[:, j:j + 1],
                bshalo[:, 0:1], ALU.mult, ALU.add))
        s.op("dve", lambda eng: eng.tensor_scalar(
            bshalo[:, 1:2], rxa[:, 16:17], maskn[:, 0:1], None, ALU.mult))
        for j in range(1, 8):
            s.op("dve", lambda eng, j=j: eng.scalar_tensor_tensor(
                bshalo[:, 1:2], rxa[:, j * 18 + 16:j * 18 + 17], maskn[:, j:j + 1],
                bshalo[:, 1:2], ALU.mult, ALU.add))
        ev_halo = s.sig("dve", "dve_c")
        return ev_xq, ev_halo

    def emit_ffn(ev_ready, first):
        ev_xq, ev_halo = emit_ffn_prep(ev_ready)
        s.aft("dve", ev_halo)
        s.op("dve", lambda eng: eng.memset(r_q2[:, :], 0))
        s.op("dve", lambda eng: eng.memset(r_b2[:, :], 0))
        ev_q2z = s.sig("dve", "dve_c")

        ev_wkfree = [None, None]
        ev_ph_free = None
        ev_psfree = [None, None, None, None]
        ev_sact_spill = {}
        ev_mini_ready = None
        for t in range(NLT):
            for g in range(HC // 4):
                bi = (t * (HC // 4) + g) % 2
                s.aft("sp", ev_wkfree[bi])
                if t == 0 and g == 0:
                    s.aft("sp", s.mark("pe_c"))
                evk = None
                for k in range(DC):
                    evk = dma("sp", wk1[bi][:, k * 512:(k + 1) * 512],
                              w1q_hbm[k][:, 4 * g * 128:(4 * g + 4) * 128],
                              "wk1_e" if bi == 0 else "wk1_o")
                s.aft("pe", evk)
                if t == 0 and g == 0:
                    s.aft("pe", ev_xq, ev_halo, ev_w1q)
                for fi in range(4):
                    s.aft("pe", ev_psfree[fi])
                    for k in range(DC):
                        s.op("pe", lambda eng, k=k, fi=fi, g=g, t=t, bi=bi: eng.matmul(
                            p_mm[fi][:, :LT],
                            wk1[bi][:, (k * 4 + fi) * 128:(k * 4 + fi + 1) * 128],
                            xq_ext[k][:, 2 + t * LT:2 + (t + 1) * LT],
                            start=(k == 0), stop=(k == DC - 1)))
                if t == 0:
                    s.aft("pe", ev_ph_free)
                    for fi in range(4):
                        for k in range(DC):
                            s.op("pe", lambda eng, k=k, fi=fi, bi=bi: eng.matmul(
                                p_h[:, fi * 2:(fi + 1) * 2],
                                wk1[bi][:, (k * 4 + fi) * 128:(k * 4 + fi + 1) * 128],
                                xq_ext[k][:, 0:2],
                                start=(k == 0), stop=(k == DC - 1)))
                ev_mm = s.sig("pe", "pe_c")
                ev_wkfree[bi] = ev_mm
                for fi in range(4):
                    f = 4 * g + fi
                    s.aft("dve", ev_mm)
                    if t > 0:
                        pass
                    s.op("dve", lambda eng, fi=fi, t=t: eng.tensor_tensor(
                        h_t[:, :], p_mm[fi][:, :LT], r_bsc[:, t * LT:(t + 1) * LT], ALU.mult))
                    ev_psfree[fi] = s.sig("dve", "dve_c")
                    if t == 0:
                        s.aft("dve", ev_halo)
                        s.op("dve", lambda eng, fi=fi, f=f: eng.tensor_tensor(
                            halocols[f][:, :], p_h[:, fi * 2:(fi + 1) * 2], bshalo[:, :], ALU.mult))
                        ev_ph_free = s.sig("dve", "dve_c")
                    sa = sacttmp[(t * HC + f) % 2]
                    s.op("dve", lambda eng, f=f: eng.tensor_scalar(
                        convtmp[:, 0:LT], h_t[:, 0:LT], cw2[:, f:f + 1], None, ALU.mult))
                    s.op("dve", lambda eng, f=f: eng.scalar_tensor_tensor(
                        convtmp[:, 1:LT], h_t[:, 0:LT - 1], cw1[:, f:f + 1],
                        convtmp[:, 1:LT], ALU.mult, ALU.add))
                    s.op("dve", lambda eng, f=f: eng.scalar_tensor_tensor(
                        convtmp[:, 2:LT], h_t[:, 0:LT - 2], cw0[:, f:f + 1],
                        convtmp[:, 2:LT], ALU.mult, ALU.add))
                    if t == 0:
                        s.op("dve", lambda eng: eng.memset(convtmp[:, 0:1], 0))
                        s.op("dve", lambda eng, f=f: eng.scalar_tensor_tensor(
                            convtmp[:, 1:2], halocols[f][:, 0:1], cw0[:, f:f + 1],
                            convtmp[:, 1:2], ALU.mult, ALU.add))
                    else:
                        s.op("dve", lambda eng, f=f: eng.scalar_tensor_tensor(
                            convtmp[:, 0:1], prev2[f][:, 1:2], cw1[:, f:f + 1],
                            convtmp[:, 0:1], ALU.mult, ALU.add))
                        s.op("dve", lambda eng, f=f: eng.scalar_tensor_tensor(
                            convtmp[:, 0:1], prev2[f][:, 0:1], cw0[:, f:f + 1],
                            convtmp[:, 0:1], ALU.mult, ALU.add))
                        s.op("dve", lambda eng, f=f: eng.scalar_tensor_tensor(
                            convtmp[:, 1:2], prev2[f][:, 1:2], cw0[:, f:f + 1],
                            convtmp[:, 1:2], ALU.mult, ALU.add))
                    s.op("dve", lambda eng, f=f: eng.tensor_copy(
                        prev2[f][:, :], h_t[:, LT - 2:LT]))
                    if t == NLT - 1:
                        s.op("dve", lambda eng, f=f: eng.tensor_scalar(
                            minis[:, f:f + 1], h_t[:, LT - 1:LT], cw1[:, f:f + 1], None, ALU.mult))
                        s.op("dve", lambda eng, f=f: eng.scalar_tensor_tensor(
                            minis[:, f:f + 1], h_t[:, LT - 2:LT - 1], cw0[:, f:f + 1],
                            minis[:, f:f + 1], ALU.mult, ALU.add))
                        s.op("dve", lambda eng, f=f: eng.scalar_tensor_tensor(
                            minis[:, f:f + 1], halocols[f][:, 1:2], cw2[:, f:f + 1],
                            minis[:, f:f + 1], ALU.mult, ALU.add))
                    ev_conv = s.sig("dve", "dve_c")
                    s.aft("act", ev_conv)
                    if (t * HC + f) >= 2:
                        s.aft("act", ev_sact_spill.get((t * HC + f - 2)))
                    s.op("act", lambda eng, f=f, sa=sa: eng.activation(
                        sa[:, :], convtmp[:, :], AF.Silu, bias=cbias[:, f:f + 1]))
                    ev_silu = s.sig("act", "act_c")
                    lo = t * LT - 1
                    s.aft("dve", ev_silu, ev_q2z)
                    if t == 0:
                        s.op("dve", lambda eng, sa=sa: eng.tensor_tensor(
                            r_q2[:, 0:LT - 1], r_q2[:, 0:LT - 1], sa[:, 1:LT], ALU.max))
                        s.op("dve", lambda eng, sa=sa: eng.tensor_tensor(
                            r_b2[:, 0:LT - 1], r_b2[:, 0:LT - 1], sa[:, 1:LT], ALU.min))
                    else:
                        s.op("dve", lambda eng, sa=sa, lo=lo: eng.tensor_tensor(
                            r_q2[:, lo:lo + LT], r_q2[:, lo:lo + LT], sa[:, 0:LT], ALU.max))
                        s.op("dve", lambda eng, sa=sa, lo=lo: eng.tensor_tensor(
                            r_b2[:, lo:lo + LT], r_b2[:, lo:lo + LT], sa[:, 0:LT], ALU.min))
                    s.sig("dve", "dve_c")
                    s.aft("sp", s.mark("dve_c"))
                    sem_sa = "sa_e" if (t * HC + f) % 2 == 0 else "sa_o"
                    if t == 0:
                        ev_sp = dma("sp", sact_hbm[f][:, 0:LT - 1], sa[:, 1:LT], sem_sa)
                    else:
                        ev_sp = dma("sp", sact_hbm[f][:, lo:lo + LT], sa[:, 0:LT], sem_sa)
                    ev_sact_spill[t * HC + f] = ev_sp
        s.op("dve", lambda eng: eng.tensor_add(minis[:, 0:HC], minis[:, 0:HC], cbias[:, 0:HC]))
        ev_mini_conv = s.sig("dve", "dve_c")
        s.aft("act", ev_mini_conv)
        s.op("act", lambda eng: eng.activation(minis[:, 0:HC], minis[:, 0:HC], AF.Silu))
        ev_ms = s.sig("act", "act_c")
        s.aft("dve", ev_ms)
        s.op("dve", lambda eng: eng.tensor_reduce(
            thin[:, 0:1], minis[:, 0:HC], AX.X, ALU.max, apply_absolute_value=True))
        s.op("dve", lambda eng: eng.tensor_tensor(
            r_q2[:, LL - 1:LL], r_q2[:, LL - 1:LL], thin[:, 0:1], ALU.max))
        ev_q2acc_mini = s.sig("dve", "dve_c")

        ev_q2acc = s.mark("dve_c")
        s.aft("dve", ev_q2acc)
        s.op("dve", lambda eng: eng.scalar_tensor_tensor(
            r_q2[:, :], r_b2[:, :], -1.0, r_q2[:, :], ALU.mult, ALU.max))
        ev_q2acc = s.sig("dve", "dve_c")
        ev_pq2 = preduce(r_q2, 0, LL, ALU.max, ev_q2acc)
        s.aft("dve", ev_pq2)
        s.op("dve", lambda eng: eng.tensor_scalar(r_q2[0:1, :], r_q2[0:1, :], 1e-5, None, ALU.max))
        s.op("dve", lambda eng: eng.tensor_scalar(
            r_b2[0:1, :], r_q2[0:1, :], swinv[0:1, 1:2], 1.0 / 127.0, ALU.mult, ALU.mult))
        s.op("dve", lambda eng: eng.reciprocal(r_q2[0:1, :], r_q2[0:1, :]))
        s.op("dve", lambda eng: eng.tensor_scalar(r_q2[0:1, :], r_q2[0:1, :], 127.0, None, ALU.mult))
        ev_rows2r = s.sig("dve", "dve_c")
        ev_c1 = pbcast(r_q2, 0, r_q2, 0, LL, 1, ev_rows2r)
        ev_rows2 = pbcast(r_b2, 0, r_b2, 0, LL, 2, ev_rows2r, ev_c1)
        ev_xq2st = {}
        HL = LL // 2
        s.aft("sp", s.mark("sa_e"), s.mark("sa_o"))
        for f2 in range(2 * HC):
            f, h2 = f2 // 2, f2 % 2
            bi = f2 % 2
            if f2 >= 2:
                s.aft("sp", ev_xq2st[f2 - 2])
            W2 = HL if h2 == 0 else HL - 1
            ev_l = dma("sp", sback[bi][:, 0:W2], sact_hbm[f][:, h2 * HL:h2 * HL + W2], "sb_e" if bi == 0 else "sb_o")
            s.aft("dve", ev_l, ev_rows2)
            s.op("dve", lambda eng, bi=bi, h2=h2, W2=W2: eng.tensor_mul(
                sback[bi][:, 0:W2], sback[bi][:, 0:W2], r_q2[:, h2 * HL:h2 * HL + W2]))
            s.op("dve", lambda eng, bi=bi, W2=W2: eng.tensor_scalar(
                xq2r[bi][:, 0:W2], sback[bi][:, 0:W2], MAGIC, MAGIC, ALU.add, ALU.subtract))
            if h2 == 1:
                s.aft("dve", ev_q2acc_mini)
                s.op("dve", lambda eng, f=f: eng.tensor_mul(
                    thin[:, 6:7], minis[:, f:f + 1], r_q2[:, LL - 1:LL]))
                s.op("dve", lambda eng, bi=bi: eng.tensor_scalar(
                    xq2r[bi][:, HL - 1:HL], thin[:, 6:7], MAGIC, MAGIC, ALU.add, ALU.subtract))
            ev_q = s.sig("dve", "dve_c")
            s.aft("sp", ev_q)
            ev_xq2st[f2] = dma("sp", xq2_hbm[f][:, h2 * HL:(h2 + 1) * HL], xq2r[bi][:, :],
                               "x2_e" if bi == 0 else "x2_o")

        ev_last_res = None
        ev_rfree = [None] * 2
        ev_wfree = [None] * 2
        ev_ps2free = list(ev_psfree)
        s.aft("sp", s.mark("x2_e"), s.mark("x2_o"), s.mark("pe_c"))
        for t in range(NLT):
            for mg in range(2):
                for k in range(HC):
                    ri = k % 2
                    s.aft("sp", ev_rfree[ri])
                    evr = dma("sp", xr2[ri][:, :], xq2_hbm[k][:, t * LT:(t + 1) * LT], f"xr_{ri}")
                    s.aft("pool", ev_wfree[ri])
                    evw = dma("pool", wk2[ri][:, :],
                              w2q_hbm[k][:, 4 * mg * 128:(4 * mg + 4) * 128],
                              f"wkb_{ri}")
                    s.aft("pe", evr, evw)
                    if t == 0 and mg == 0 and k == 0:
                        s.aft("pe", ev_w2q)
                    for mi in range(4):
                        if k == 0:
                            s.aft("pe", ev_ps2free[mi])
                        s.op("pe", lambda eng, k=k, mi=mi, ri=ri: eng.matmul(
                            p_mm[mi][:, :LT],
                            wk2[ri][:, mi * 128:(mi + 1) * 128],
                            xr2[ri][:, :],
                            start=(k == 0), stop=(k == HC - 1)))
                    ev = s.sig("pe", "pe_c")
                    ev_rfree[ri] = ev
                    ev_wfree[ri] = ev
                ev_mm2 = s.mark("pe_c")
                for mi in range(4):
                    m = 4 * mg + mi
                    s.aft("dve", ev_mm2)
                    s.op("dve", lambda eng, mi=mi, t=t: eng.tensor_tensor(
                        convtmp[:, :], p_mm[mi][:, :LT], r_b2[:, t * LT:(t + 1) * LT], ALU.mult))
                    s.op("dve", lambda eng, m=m, t=t: eng.tensor_add(
                        resid[m][:, t * LT:(t + 1) * LT], resid[m][:, t * LT:(t + 1) * LT],
                        convtmp[:, :]))
                    ev_ps2free[mi] = s.sig("dve", "dve_c")
                    ev_last_res = ev_ps2free[mi]
        return ev_last_res


    MSP = 512
    NSPAN = L // MSP
    MDC = 8
    NST = 16
    PAIRS = [[2 * b_, 2 * b_ + 1] for b_ in range(4)]

    m_alog = sb([128, 128])
    m_cw = sb([128, 32]); m_cbias = sb([128, 8])
    m_dtb = sb([128, 8]); m_dvec = sb([128, 8])
    m_state = sb([128, 128])
    m_prev3 = [sb([128, 3]) for _ in range(MDC)]
    m_seln = sb([16, 128])
    m_hsel = sb([128, 8])
    m_dt = ([R0[:, j * MSP:(j + 1) * MSP] for j in range(4)]
            + [R1[:, j * MSP:(j + 1) * MSP] for j in range(4)])
    m_v = ([R2[:, j * MSP:(j + 1) * MSP] for j in range(4)]
           + [R3[:, j * MSP:(j + 1) * MSP] for j in range(4)])
    m_y = ([R4[:, j * MSP:(j + 1) * MSP] for j in range(4)]
           + [sback[0][:, 0:MSP], sback[0][:, MSP:2 * MSP],
              sback[1][:, 0:MSP], sback[1][:, MSP:2 * MSP]])
    m_xbsp = ([wstream[0][:, j * MSP:(j + 1) * MSP] for j in range(4)]
              + [wstream[1][:, j * MSP:(j + 1) * MSP] for j in range(4)])
    m_lnsp = [wk1[0][:, j * MSP:(j + 1) * MSP] for j in range(8)]
    m_xbb = [wk1[1][:, j * MSP:(j + 1) * MSP] for j in range(8)]
    m_g = [xq_ext[j][:, 0:MSP] for j in range(8)]
    m_rh = xr2
    m_wkbuf = [xq_ext[4][:, 1026:2050], xq_ext[5][:, 1026:2050]]
    m_xpw = xq_ext[6][:, 770:2050]
    m_dtw = xq_ext[7][:, 1026:2050]
    m_stg = convtmp
    m_dec = h_t
    m_inp = sacttmp[0]
    m_h = sacttmp[1]
    m_hc = h_t
    m_xdt = xq2r[0][:, 0:MSP]
    m_bc = sb([16, 2 * MSP])
    m_browb = m_bc[:, 0:MSP]; m_crowb = m_bc[:, MSP:2 * MSP]
    m_ldbuf = sacttmp[0]
    m_xbld = m_ldbuf; m_zbld = m_ldbuf

    def emit_mamba(ev_ready, tag):
        q = 100 + tag
        xb_hbm = nc.dram_tensor(f"mxb{q}", [NSPAN, MDC, 128, MSP], F32, kind="Internal").ap()
        zb_hbm = nc.dram_tensor(f"mzb{q}", [NSPAN, MDC, 128, MSP], F32, kind="Internal").ap()
        wxz_bf = nc.dram_tensor(f"mwxz{q}", [16, 128, 1024], BF16, kind="Internal").ap()
        ln_in = nc.dram_tensor(f"mlnin{q}", [DC, 128, LL], BF16, kind="Internal")
        ln_out = nc.dram_tensor(f"mlnout{q}", [2, DC, 128, LL], BF16, kind="Internal")
        xd_in = nc.dram_tensor(f"mxdin{q}", [160, L], F32, kind="Internal")
        xd_out = nc.dram_tensor(f"mxdout{q}", [160, L], F32, kind="Internal")
        g_in = nc.dram_tensor(f"mgin{q}", [NSPAN, MDC, 128, MSP], BF16, kind="Internal")
        g_out = nc.dram_tensor(f"mgout{q}", [2, NSPAN, MDC, 128, MSP], BF16, kind="Internal")

        evq = (s.mark("dve_c"), s.mark("act_c"), s.mark("pe_c"))
        emit_mamba.ev_stg = None
        s.aft("pool", evq[2])

        if tag == 0:
            for hh, dst in ((alog, m_alog), (mcw, m_cw), (mcb, m_cbias),
                            (mdtb, m_dtb), (mdc, m_dvec)):
                dma("pool", dst[:, :], hh, "ld_const")
            dma("pool", m_hsel[0:1, :], halfsel, "ld_const")
            ev_c = s.mark("ld_const")
            emit_mamba.ev_hsel = pbcast(m_hsel, 0, m_hsel, 0, 8, 0, ev_c)
            s.aft("act", ev_c)
            s.op("act", lambda eng: eng.activation(m_alog[:, :], m_alog[:, :], AF.Exp))
            ev_e = s.sig("act", "act_c")
            s.aft("dve", ev_e)
            s.op("dve", lambda eng: eng.tensor_scalar(
                m_alog[:, :], m_alog[:, :], -1.0, None, ALU.mult))
            emit_mamba.ev_alog = s.sig("dve", "dve_c")
            for mi in range(16):
                w_ap = inpw_x if mi < 8 else inpw_z
                mm = mi % 8
                for k in range(DC):
                    ev = dma("pool", m_wkbuf[0][:, k * 128:(k + 1) * 128],
                             w_ap.rearrange("(k p) m -> k p m", p=128)[k][:, mm * 128:(mm + 1) * 128],
                             "m_wc")
                s.aft("sp", ev, s.mark("m_wcs"))
                ev2 = dma("sp", wxz_bf[mi], m_wkbuf[0][:, :], "m_wcs")
                s.pending["pool"].append(ev2)
            emit_mamba.ev_wcast = s.mark("m_wcs")
            for k in range(DC):
                dma("pool", m_xpw[:, k * 160:(k + 1) * 160],
                    xpw.rearrange("(k p) m -> k p m", p=128)[k], "ld_const")
            dma("pool", m_dtw[:, :], dtpw, "ld_const")
            emit_mamba.ev_w2 = s.mark("ld_const")
        ev_alog = emit_mamba.ev_alog
        ev_wcast = emit_mamba.ev_wcast
        ev_wres = emit_mamba.ev_w2

        ev_mm = None
        s.aft("act", s.mark("st_wq1"), s.mark("st_wq2"))
        for c in range(DC):
            s.aft("act", ev_ready)
            if ev_mm is not None:
                s.aft("act", ev_mm)
            s.op("act", lambda eng, c=c: eng.activation(castbuf[:, :], resid[c][:, :], AF.Square))
            ev_cs = s.sig("act", "act_c")
            for nt in range(NLT):
                s.aft("pe", ev_cs)
                if c == 0:
                    s.aft("pe", ev_small, ev_ready, emit_mamba.ev_hsel)
                s.op("pe", lambda eng, c=c, nt=nt: eng.matmul(
                    p_big[0:1, nt * 512:(nt + 1) * 512], onescol[:, :],
                    resid[c][:, nt * 512:(nt + 1) * 512],
                    start=(c == 0), stop=(c == DC - 1)))
                s.op("pe", lambda eng, c=c, nt=nt: eng.matmul(
                    p_mm[nt][0:1, :], lnones[:, 0:1],
                    castbuf[:, nt * 512:(nt + 1) * 512],
                    start=(c == 0), stop=(c == DC - 1)))
            ev_mm = s.sig("pe", "pe_c")
        s.aft("dve", ev_mm)
        s.op("dve", lambda eng: eng.tensor_scalar(
            r_mean[0:1, :], p_big[0:1, :], 1.0 / D, None, ALU.mult))
        for nt in range(NLT):
            s.op("dve", lambda eng, nt=nt: eng.tensor_scalar(
                r_rstd[0:1, nt * 512:(nt + 1) * 512], p_mm[nt][0:1, :], 1.0 / D, None, ALU.mult))
        s.op("dve", lambda eng: eng.tensor_mul(r_b[0:1, :], r_mean[0:1, :], r_mean[0:1, :]))
        s.op("dve", lambda eng: eng.tensor_sub(r_rstd[0:1, :], r_rstd[0:1, :], r_b[0:1, :]))
        s.op("dve", lambda eng: eng.tensor_scalar(r_rstd[0:1, :], r_rstd[0:1, :], 1e-5, None, ALU.add))
        s.op("dve", lambda eng: eng.reciprocal(r_rstd[0:1, :], r_rstd[0:1, :]))
        ev_vv = s.sig("dve", "dve_c")
        s.aft("act", ev_vv)
        s.op("act", lambda eng: eng.activation(r_rstd[0:1, :], r_rstd[0:1, :], AF.Sqrt))
        ev_r = s.sig("act", "act_c")
        ev_b1 = pbcast(r_mean, 0, r_mean, 0, LL, 0, ev_r)
        ev_rows = pbcast(r_rstd, 0, r_rstd, 0, LL, 1, ev_r, ev_b1)
        ev_lnst = None
        for c in range(DC):
            s.aft("dve", ev_rows, ev_lnst)
            s.op("dve", lambda eng, c=c: eng.tensor_sub(r_b[:, :], resid[c][:, :], r_mean[:, :]))
            s.op("dve", lambda eng, c=c: eng.tensor_mul(r_b[:, :], r_b[:, :], r_rstd[:, :]))
            s.op("dve", lambda eng, c=c: eng.tensor_copy(castbuf[:, :], r_b[:, :]))
            ev_l = s.sig("dve", "dve_c")
            s.aft("sp", ev_l)
            ev_lnst = dma("sp", ln_in.ap()[c], castbuf[:, :], "m_lnst")
        s.aft("pool", ev_lnst)
        s.op("pool", lambda eng: eng.collective_compute(
            "AllGather", ALU.bypass, ins=[ln_in.ap()], outs=[ln_out.ap()],
            replica_groups=PAIRS), isdma=True)
        ev_ln_ag = s.sig("pool", "ccs")

        ev_wkfree = [None, None]
        ev_pmfree = [None] * 4
        ev_xbb_free = None
        ev_xd_done = None
        for sp in range(NSPAN):
            slot = sp // (NSPAN // 2)
            lcol = (sp % (NSPAN // 2)) * MSP
            s.aft("pool", ev_ln_ag if sp == 0 else None)
            s.aft("pool", s.mark("pe_c") if sp > 0 else None)
            evln = None
            for c in range(DC):
                evln = dma("pool", m_lnsp[c], ln_out.ap()[slot, c][:, lcol:lcol + MSP], "m_lnld")
            for mi in range(16):
                bi = mi % 2
                s.aft("sp", ev_wkfree[bi], ev_wcast if sp == 0 and mi < 2 else None)
                evw = dma("sp", m_wkbuf[bi][:, :], wxz_bf[mi], "m_wk_e" if bi == 0 else "m_wk_o")
                pm = mi % 4
                for k in range(DC):
                    s.aft("pe", evw, evln, ev_pmfree[pm] if k == 0 else None)
                    if sp == 0 and mi == 0 and k == 0:
                        s.aft("pe", evq[0])
                    s.op("pe", lambda eng, mi=mi, k=k, bi=bi, pm=pm: eng.matmul(
                        p_mm[pm][:, :MSP],
                        m_wkbuf[bi][:, k * 128:(k + 1) * 128],
                        m_lnsp[k],
                        start=(k == 0), stop=(k == DC - 1)))
                ev_g = s.sig("pe", "pe_c")
                ev_wkfree[bi] = ev_g
                dc = mi % 8
                if mi < 8:
                    s.aft("dve", ev_g, emit_mamba.ev_stg)
                    if sp == 0:
                        s.op("dve", lambda eng, dc=dc: eng.memset(m_prev3[dc][:, :], 0))
                    s.op("dve", lambda eng, pm=pm, dc=dc: eng.tensor_scalar(
                        m_stg[:, 0:MSP], p_mm[pm][:, :MSP], m_cw[:, dc * 4 + 3:dc * 4 + 4],
                        None, ALU.mult))
                    s.op("dve", lambda eng, pm=pm, dc=dc: eng.scalar_tensor_tensor(
                        m_stg[:, 1:MSP], p_mm[pm][:, 0:MSP - 1], m_cw[:, dc * 4 + 2:dc * 4 + 3],
                        m_stg[:, 1:MSP], ALU.mult, ALU.add))
                    s.op("dve", lambda eng, pm=pm, dc=dc: eng.scalar_tensor_tensor(
                        m_stg[:, 2:MSP], p_mm[pm][:, 0:MSP - 2], m_cw[:, dc * 4 + 1:dc * 4 + 2],
                        m_stg[:, 2:MSP], ALU.mult, ALU.add))
                    s.op("dve", lambda eng, pm=pm, dc=dc: eng.scalar_tensor_tensor(
                        m_stg[:, 3:MSP], p_mm[pm][:, 0:MSP - 3], m_cw[:, dc * 4 + 0:dc * 4 + 1],
                        m_stg[:, 3:MSP], ALU.mult, ALU.add))
                    for w_ in range(3):
                        for t_ in range(w_ + 1 if False else 3 - w_):
                            pass
                    s.op("dve", lambda eng, dc=dc: eng.scalar_tensor_tensor(
                        m_stg[:, 0:1], m_prev3[dc][:, 2:3], m_cw[:, dc * 4 + 2:dc * 4 + 3],
                        m_stg[:, 0:1], ALU.mult, ALU.add))
                    s.op("dve", lambda eng, dc=dc: eng.scalar_tensor_tensor(
                        m_stg[:, 0:1], m_prev3[dc][:, 1:2], m_cw[:, dc * 4 + 1:dc * 4 + 2],
                        m_stg[:, 0:1], ALU.mult, ALU.add))
                    s.op("dve", lambda eng, dc=dc: eng.scalar_tensor_tensor(
                        m_stg[:, 0:1], m_prev3[dc][:, 0:1], m_cw[:, dc * 4 + 0:dc * 4 + 1],
                        m_stg[:, 0:1], ALU.mult, ALU.add))
                    s.op("dve", lambda eng, dc=dc: eng.scalar_tensor_tensor(
                        m_stg[:, 1:2], m_prev3[dc][:, 2:3], m_cw[:, dc * 4 + 1:dc * 4 + 2],
                        m_stg[:, 1:2], ALU.mult, ALU.add))
                    s.op("dve", lambda eng, dc=dc: eng.scalar_tensor_tensor(
                        m_stg[:, 1:2], m_prev3[dc][:, 1:2], m_cw[:, dc * 4 + 0:dc * 4 + 1],
                        m_stg[:, 1:2], ALU.mult, ALU.add))
                    s.op("dve", lambda eng, dc=dc: eng.scalar_tensor_tensor(
                        m_stg[:, 2:3], m_prev3[dc][:, 2:3], m_cw[:, dc * 4 + 0:dc * 4 + 1],
                        m_stg[:, 2:3], ALU.mult, ALU.add))
                    s.op("dve", lambda eng, pm=pm, dc=dc: eng.tensor_copy(
                        m_prev3[dc][:, :], p_mm[pm][:, MSP - 3:MSP]))
                    ev_conv = s.sig("dve", "dve_c")
                    ev_pmfree[pm] = ev_conv
                    s.aft("act", ev_conv)
                    s.op("act", lambda eng, dc=dc: eng.activation(
                        m_stg[:, 0:MSP], m_stg[:, 0:MSP], AF.Silu, bias=m_cbias[:, dc:dc + 1]))
                    ev_silu = s.sig("act", "act_c")
                    s.aft("dve", ev_silu, ev_xbb_free if mi == 0 else None)
                    s.op("dve", lambda eng, dc=dc: eng.tensor_copy(m_xbb[dc], m_stg[:, 0:MSP]))
                    ev_bb = s.sig("dve", "dve_c")
                    s.aft("sp", ev_bb)
                    ev_sp_xb = dma("sp", xb_hbm[sp, dc], m_stg[:, 0:MSP],
                                   "m_xbst_e" if (sp * 8 + dc) % 2 == 0 else "m_xbst_o")
                    emit_mamba.ev_stg = ev_sp_xb
                else:
                    s.aft("dve", ev_g, emit_mamba.ev_stg)
                    s.op("dve", lambda eng, pm=pm: eng.tensor_copy(m_stg[:, 0:MSP], p_mm[pm][:, :MSP]))
                    ev_z = s.sig("dve", "dve_c")
                    ev_pmfree[pm] = ev_z
                    s.aft("sp", ev_z)
                    ev_sp_zb = dma("sp", zb_hbm[sp, dc], m_stg[:, 0:MSP],
                                   "m_zbst_e" if (sp * 8 + dc) % 2 == 0 else "m_zbst_o")
                    emit_mamba.ev_stg = ev_sp_zb
            ev_xbb_all = s.mark("dve_c")
            for k in range(DC):
                s.aft("pe", ev_xbb_all if k == 0 else None, ev_wres if sp == 0 and k == 0 else None)
                s.op("pe", lambda eng, k=k: eng.matmul(
                    p_big[:, 0:MSP], m_xpw[:, k * 160:k * 160 + 128], m_xbb[k],
                    start=(k == 0), stop=(k == DC - 1)))
                s.op("pe", lambda eng, k=k: eng.matmul(
                    p_big[0:16, 512:512 + MSP], m_xpw[:, k * 160 + 128:k * 160 + 144], m_xbb[k],
                    start=(k == 0), stop=(k == DC - 1)))
                s.op("pe", lambda eng, k=k: eng.matmul(
                    p_big[0:16, 1024:1024 + MSP], m_xpw[:, k * 160 + 144:k * 160 + 160], m_xbb[k],
                    start=(k == 0), stop=(k == DC - 1)))
            ev_xp = s.sig("pe", "pe_c")
            ev_xbb_free = ev_xp
            s.aft("dve", ev_xp, emit_mamba.ev_stg)
            s.op("dve", lambda eng: eng.tensor_copy(m_stg[:, 0:MSP], p_big[:, 0:MSP]))
            ev_e1 = s.sig("dve", "dve_c")
            s.aft("sp", ev_e1)
            ev_s1 = dma("sp", xd_in.ap()[0:128, sp * MSP:(sp + 1) * MSP], m_stg[:, 0:MSP], "m_xdst")
            s.aft("dve", ev_s1)
            emit_mamba.ev_stg = ev_s1
            s.op("dve", lambda eng: eng.tensor_copy(m_stg[0:16, 0:MSP], p_big[0:16, 512:512 + MSP]))
            ev_e2 = s.sig("dve", "dve_c")
            s.aft("sp", ev_e2)
            ev_s2 = dma("sp", xd_in.ap()[128:144, sp * MSP:(sp + 1) * MSP], m_stg[0:16, 0:MSP], "m_xdst")
            s.aft("dve", ev_s2)
            emit_mamba.ev_stg = ev_s2
            s.op("dve", lambda eng: eng.tensor_copy(m_stg[0:16, 0:MSP], p_big[0:16, 1024:1024 + MSP]))
            ev_e3 = s.sig("dve", "dve_c")
            s.aft("sp", ev_e3)
            ev_s3 = dma("sp", xd_in.ap()[144:160, sp * MSP:(sp + 1) * MSP], m_stg[0:16, 0:MSP], "m_xdst")
            s.aft("dve", ev_s3)
            emit_mamba.ev_stg = ev_s3
            ev_xd_done = ev_s3
        s.aft("pool", s.mark("m_xdst"))
        s.op("pool", lambda eng: eng.collective_compute(
            "AllReduce", ALU.add, ins=[xd_in.ap()], outs=[xd_out.ap()],
            replica_groups=PAIRS), isdma=True)
        ev_xd_ar = s.sig("pool", "ccs")

        s.aft("dve", s.mark("dve_c"), evq[2])
        s.op("dve", lambda eng: eng.memset(m_state[:, :], 0))
        ev_st0 = s.sig("dve", "dve_c")
        ev_gst = {}
        for sp in range(NSPAN):
            s.aft("pool", ev_xd_ar if sp == 0 else None, s.mark("dve_c"), s.mark("pe_c"))
            ev_xdt = dma("pool", m_xdt, xd_out.ap()[0:128, sp * MSP:(sp + 1) * MSP], "m_bld")
            ev_br = dma("pool", m_browb[:, :], xd_out.ap()[128:144, sp * MSP:(sp + 1) * MSP], "m_bld")
            ev_cr = dma("pool", m_crowb[:, :], xd_out.ap()[144:160, sp * MSP:(sp + 1) * MSP], "m_bld")
            ev_bld = ev_cr
            for dc in range(MDC):
                s.aft("pe", ev_bld, ev_wres if sp == 0 and dc == 0 else None)
                s.op("pe", lambda eng, dc=dc: eng.matmul(
                    p_big[:, 0:MSP], m_dtw[:, dc * 128:(dc + 1) * 128], m_xdt,
                    start=True, stop=True))
                ev_dtm = s.sig("pe", "pe_c")
                s.aft("act", ev_dtm, emit_mamba.ev_stg)
                s.op("act", lambda eng, dc=dc: eng.activation(
                    m_stg[:, 0:MSP], p_big[:, 0:MSP], AF.Exp, bias=m_dtb[:, dc:dc + 1]))
                ev_ex = s.sig("act", "act_c")
                s.aft("act", ev_ex)
                s.op("act", lambda eng, dc=dc: eng.activation(
                    m_dt[dc], m_stg[:, 0:MSP], AF.Ln, bias=1.0))
                ev_dt = s.sig("act", "act_c")
                s.aft("pool", s.mark("dve_c"))
                evx = dma("pool", m_xbld[:, :], xb_hbm[sp, dc], "m_xbld")
                s.aft("dve", evx, ev_dt)
                s.op("dve", lambda eng, dc=dc: eng.tensor_mul(m_v[dc], m_dt[dc], m_xbld[:, :]))
                s.op("dve", lambda eng, dc=dc: eng.tensor_copy(m_xbsp[dc], m_xbld[:, :]))
                ev_vv2 = s.sig("dve", "dve_c")
                s.aft("pool", ev_vv2)
                s.aft("pe", ev_vv2)
            for n in range(NST):
                s.aft("pool", s.mark("pe_c"))
                ev_sel = dma("pool", m_seln[:, :], selnf[:, n * 128:(n + 1) * 128], "m_selld")
                s.aft("pe", ev_br, ev_cr, ev_sel)
                s.op("pe", lambda eng, n=n: eng.matmul(
                    p_big[:, 512:512 + MSP], m_seln[:, :], m_browb[:, :],
                    start=True, stop=True))
                s.op("pe", lambda eng, n=n: eng.matmul(
                    p_big[:, 1024:1024 + MSP], m_seln[:, :], m_crowb[:, :],
                    start=True, stop=True))
                ev_bc = s.sig("pe", "pe_c")
                for dc in range(MDC):
                    col = dc * 16 + n
                    s.aft("act", ev_bc if dc == 0 else None,
                          ev_alog if sp == 0 and n == 0 and dc == 0 else None,
                          s.mark("dve_c"))
                    s.op("act", lambda eng, dc=dc, col=col: eng.activation(
                        m_dec[:, 0:MSP], m_dt[dc], AF.Exp, scale=m_alog[:, col:col + 1]))
                    ev_de = s.sig("act", "act_c")
                    s.aft("dve", ev_de, ev_bc, ev_st0 if sp == 0 else None)
                    s.op("dve", lambda eng, dc=dc: eng.scalar_tensor_tensor(
                        m_inp[:, 0:MSP], m_v[dc], 1.0, p_big[:, 512:512 + MSP],
                        ALU.mult, ALU.mult))
                    s.op("dve", lambda eng, dc=dc, col=col: eng.tensor_tensor_scan(
                        m_h[:, 0:MSP], m_dec[:, 0:MSP], m_inp[:, 0:MSP],
                        m_state[:, col:col + 1], ALU.mult, ALU.add))
                    s.op("dve", lambda eng, col=col: eng.tensor_copy(
                        m_state[:, col:col + 1], m_h[:, MSP - 1:MSP]))
                    if n == 0:
                        s.op("dve", lambda eng, dc=dc: eng.scalar_tensor_tensor(
                            m_y[dc], m_h[:, 0:MSP], 1.0, p_big[:, 1024:1024 + MSP],
                            ALU.mult, ALU.mult))
                    else:
                        s.op("dve", lambda eng, dc=dc: eng.scalar_tensor_tensor(
                            m_hc[:, 0:MSP], m_h[:, 0:MSP], 1.0, p_big[:, 1024:1024 + MSP],
                            ALU.mult, ALU.mult))
                        s.op("dve", lambda eng, dc=dc: eng.tensor_add(
                            m_y[dc], m_y[dc], m_hc[:, 0:MSP]))
                    ev_scan = s.sig("dve", "dve_c")
                s.aft("pe", ev_scan)
            for dc in range(MDC):
                s.aft("pool", s.mark("dve_c") if dc == 0 else None)
                evz = dma("pool", m_zbld[:, :], zb_hbm[sp, dc], "m_zbld")
                s.aft("act", evz)
                s.op("act", lambda eng: eng.activation(m_zbld[:, :], m_zbld[:, :], AF.Silu))
                ev_zs = s.sig("act", "act_c")
                s.aft("dve", ev_zs, ev_gst.get((sp - 1, dc)))
                s.op("dve", lambda eng, dc=dc: eng.scalar_tensor_tensor(
                    m_hc[:, 0:MSP], m_xbsp[dc], m_dvec[:, dc:dc + 1], m_y[dc],
                    ALU.mult, ALU.add))
                s.op("dve", lambda eng, dc=dc: eng.tensor_mul(m_g[dc], m_hc[:, 0:MSP], m_zbld[:, :]))
                ev_gg = s.sig("dve", "dve_c")
                s.aft("pool", ev_gg)
                s.aft("sp", ev_gg)
                ev_gst[(sp, dc)] = dma("sp", g_in.ap()[sp, dc], m_g[dc], "m_gst")
                s.aft("dve", ev_gst[(sp, dc)])
        s.aft("pool", s.mark("m_gst"))
        s.op("pool", lambda eng: eng.collective_compute(
            "AllGather", ALU.bypass, ins=[g_in.ap()], outs=[g_out.ap()],
            replica_groups=PAIRS), isdma=True)
        ev_a2a = s.sig("pool", "ccs")

        ev_rhfree = [None] * 2
        ev_wkf = [None, None]
        ev_opfree = [None] * 8
        ev_last = None
        opw_t = opw.rearrange("(k p) m -> k p m", p=128)
        for sl in range(NSPAN):
            for kk in range(16):
                slot, dcs = kk // 8, kk % 8
                ri = kk % 2
                s.aft("pool", ev_rhfree[ri], ev_a2a if sl == 0 and kk == 0 else None)
                evr = dma("pool", m_rh[ri][:, :], g_out.ap()[slot, sl, dcs], f"m_rh{ri}")
                bi = kk % 2
                s.aft("pool", ev_wkf[bi])
                evw = dma("pool", m_wkbuf[bi][:, :], opw_t[kk], "m_opw_e" if bi == 0 else "m_opw_o")
                for m in range(8):
                    ps = p_mm[m] if m < 4 else None
                    s.aft("pe", evr, evw)
                    if kk == 0:
                        s.aft("pe", ev_opfree[m])
                    if m < 4:
                        s.op("pe", lambda eng, kk=kk, m=m, ri=ri, bi=bi: eng.matmul(
                            p_mm[m][:, :MSP], m_wkbuf[bi][:, m * 128:(m + 1) * 128],
                            m_rh[ri][:, :], start=(kk == 0), stop=(kk == 15)))
                    else:
                        s.op("pe", lambda eng, kk=kk, m=m, ri=ri, bi=bi: eng.matmul(
                            p_big[:, (m - 4) * 512:(m - 4) * 512 + MSP],
                            m_wkbuf[bi][:, m * 128:(m + 1) * 128],
                            m_rh[ri][:, :], start=(kk == 0), stop=(kk == 15)))
                ev_k = s.sig("pe", "pe_c")
                ev_rhfree[ri] = ev_k
                ev_wkf[bi] = ev_k
            ev_op = s.mark("pe_c")
            lc = (sl % 4) * MSP
            for m in range(8):
                s.aft("dve", ev_op, emit_mamba.ev_hsel if sl == 0 and m == 0 else None)
                src_ps = (lambda m: (p_mm[m][:, :MSP] if m < 4 else
                                     p_big[:, (m - 4) * 512:(m - 4) * 512 + MSP]))(m)
                s.op("dve", lambda eng, m=m, lc=lc, sl=sl, src_ps=src_ps: eng.scalar_tensor_tensor(
                    resid[m][:, lc:lc + MSP], src_ps, m_hsel[:, sl:sl + 1],
                    resid[m][:, lc:lc + MSP], ALU.mult, ALU.add))
                ev_last = s.sig("dve", "dve_c")
                ev_opfree[m] = ev_last
        return ev_last

    _APH = int(_os.environ.get("ATTN_PHASES", "9"))

    def emit_attn_a(ev_ready):
        ag1_in = nc.dram_tensor("ag1i", [16, 128, 1024], BF16, kind="Internal")
        ag1_out = nc.dram_tensor("ag1o", [8, 16, 128, 1024], BF16, kind="Internal",
                                 addr_space="Shared")
        xtm_hbm = nc.dram_tensor("axtm", [32, 128, 512], BF16, kind="Internal").ap()
        x_hbm = nc.dram_tensor("axsp", [66, 128, 512], F32, kind="Internal").ap()
        sk_hbm = nc.dram_tensor("asksp", [66, 128, 128], F32, kind="Internal").ap()
        y_hbm = nc.dram_tensor("aysp", [66, 128, 512], BF16, kind="Internal").ap()
        ln0_hbm = nc.dram_tensor("aln0", [DC, 128, LL], F32, kind="Internal").ap()
        ag2_in = nc.dram_tensor("ag2i", [128, 16384], BF16, kind="Internal")
        ag2_out = nc.dram_tensor("ag2o", [8, 128, 16384], BF16, kind="Internal",
                                 addr_space="Shared")

        sidf = sback[0][:, 0:128]

        def swabc(i):
            return sback[0][:, 128 + i:129 + i]
        sagx1 = sb([128, 128], mybir.dt.int32)
        sagx2 = sb([128, 32], mybir.dt.int32)

        def kt(s_):
            return wstream[s_ // 16][0:32, (s_ % 16) * 128:(s_ % 16 + 1) * 128]

        dma("pool", sidf, identf_in, "a_ld")
        dma("pool", sagx1[:, :], agidx1_in, "a_ld")
        dma("pool", sagx2[:, :], agidx2_in, "a_ld")
        dma("pool", sback[0][:, 128:144], wab, "a_ld")
        s.aft("pool", s.mark("st_wq1"), s.mark("st_wq2"))
        dma("pool", kt(27), care_t, "a_ld")
        dma("pool", kt(28), cimt, "a_ld")
        dma("pool", kt(29), lare_t, "a_ld")
        dma("pool", kt(26), aimt, "a_ld")
        dma("pool", convtmp[0:1, 0:128], ldt_r, "a_ld")
        ev_ld = dma("pool", h_t[0:1, 0:512], dtile_in, "a_ld")

        ev_mm = None
        for c in range(DC):
            s.aft("act", ev_ready)
            if ev_mm is not None:
                s.aft("act", ev_mm)
            s.op("act", lambda eng, c=c: eng.activation(
                xq_ext[0][:, 0:LL], resid[c][:, :], AF.Square))
            ev_cs = s.sig("act", "act_c")
            for nt in range(NLT):
                s.aft("pe", ev_cs)
                if c == 0:
                    s.aft("pe", ev_small, ev_ready, ev_masks)
                s.op("pe", lambda eng, c=c, nt=nt: eng.matmul(
                    p_big[0:1, nt * 512:(nt + 1) * 512], onescol[:, :],
                    resid[c][:, nt * 512:(nt + 1) * 512],
                    start=(c == 0), stop=(c == DC - 1)))
                s.op("pe", lambda eng, c=c, nt=nt: eng.matmul(
                    p_mm[nt][0:1, :], lnones[:, 0:1],
                    xq_ext[0][:, nt * 512:(nt + 1) * 512],
                    start=(c == 0), stop=(c == DC - 1)))
            ev_mm = s.sig("pe", "pe_c")
        s.aft("dve", ev_mm)
        s.op("dve", lambda eng: eng.tensor_scalar(
            r_mean[0:1, :], p_big[0:1, :], 1.0 / D, None, ALU.mult))
        for nt in range(NLT):
            s.op("dve", lambda eng, nt=nt: eng.tensor_scalar(
                r_rstd[0:1, nt * 512:(nt + 1) * 512], p_mm[nt][0:1, :], 1.0 / D, None, ALU.mult))
        s.op("dve", lambda eng: eng.tensor_mul(r_b[0:1, :], r_mean[0:1, :], r_mean[0:1, :]))
        s.op("dve", lambda eng: eng.tensor_sub(r_rstd[0:1, :], r_rstd[0:1, :], r_b[0:1, :]))
        s.op("dve", lambda eng: eng.tensor_scalar(r_rstd[0:1, :], r_rstd[0:1, :], 1e-5, None, ALU.add))
        s.op("dve", lambda eng: eng.reciprocal(r_rstd[0:1, :], r_rstd[0:1, :]))
        ev_vv = s.sig("dve", "dve_c")
        s.aft("act", ev_vv)
        s.op("act", lambda eng: eng.activation(r_rstd[0:1, :], r_rstd[0:1, :], AF.Sqrt))
        ev_r = s.sig("act", "act_c")
        ev_b1 = pbcast(r_mean, 0, r_mean, 0, LL, 0, ev_r)
        ev_rows = pbcast(r_rstd, 0, r_rstd, 0, LL, 1, ev_r, ev_b1)
        ev_ldtb = pbcast(convtmp, 0, convtmp, 0, 128, 2, ev_rows, ev_ld)
        ev_evac = [None] * 4
        ev_t1st = [None] * 4
        ev_pe_c = None
        ev_lnst = None
        for c in range(DC):
            s.aft("dve", ev_rows, ev_pe_c, ev_lnst)
            s.op("dve", lambda eng, c=c: eng.tensor_sub(r_b[:, :], resid[c][:, :], r_mean[:, :]))
            s.op("dve", lambda eng, c=c: eng.tensor_mul(r_b[:, :], r_b[:, :], r_rstd[:, :]))
            ev_nrm = s.sig("dve", "dve_c")
            s.aft("sp", ev_nrm)
            ev_lnst = dma("sp", ln0_hbm[c], r_b[:, :], "a_ln0")
            for lt in range(16):
                w = lt % 4
                s.aft("pe", ev_nrm, ev_evac[w])
                if c == 0 and lt == 0:
                    s.aft("pe", s.mark("a_ld"), ev_vv)
                s.op("pe", lambda eng, c=c, lt=lt, w=w: eng.transpose(
                    p_mm[w][:, 0:128], r_b[:, lt * 128:(lt + 1) * 128], sidf))
                ev_t = s.sig("pe", "pe_c")
                s.aft("dve", ev_t, ev_t1st[w])
                s.op("dve", lambda eng, w=w: eng.tensor_copy(
                    xq_ext[2][:, w * 128:(w + 1) * 128], p_mm[w][:, 0:128]))
                ev_e = s.sig("dve", "dve_c")
                ev_evac[w] = ev_e
                s.aft("sp", ev_e)
                ev_t1st[w] = dma("sp", ag1_in.ap()[lt][:, c * 128:(c + 1) * 128],
                                 xq_ext[2][:, w * 128:(w + 1) * 128], f"a_t1_{w}")
            ev_pe_c = s.mark("pe_c")
        s.aft("pool", *[s.mark(f"a_t1_{w}") for w in range(4)])
        s.op("pool", lambda eng: eng.collective_compute(
            "AllGather", ALU.bypass, ins=[ag1_in.ap()], outs=[ag1_out.ap()],
            replica_groups=[list(range(8))]), isdma=True)
        ev_ag1 = s.sig("pool", "ccs")

        if _APH < 2:
            return ev_pe_c
        PI2 = 1.5707963267948966
        s.aft("act", s.mark("a_ld"), ev_ldtb)
        s.op("act", lambda eng: eng.activation(kt(24), convtmp[0:32, 0:128], AF.Exp))
        s.op("act", lambda eng: eng.activation(kt(30), kt(29), AF.Exp))
        ev_a = s.sig("act", "act_c")
        ev_dtileb = pbcast(h_t, 0, convtmp, 0, 512, 3, ev_a, ev_rows)
        s.aft("dve", ev_a)
        s.op("dve", lambda eng: eng.tensor_scalar(kt(25), kt(30), -1.0, None, ALU.mult))
        s.op("dve", lambda eng: eng.tensor_mul(kt(29), kt(25), kt(24)))
        s.op("dve", lambda eng: eng.tensor_mul(kt(30), kt(26), kt(24)))
        ev_d = s.sig("dve", "dve_c")
        s.aft("act", ev_d)
        s.op("act", lambda eng: eng.activation(kt(31), kt(29), AF.Exp))
        ev_a = s.sig("act", "act_c")
        s.aft("dve", ev_a)
        s.op("dve", lambda eng: eng.tensor_scalar(kt(29), kt(30), PI2, None, ALU.add))
        ev_d = s.sig("dve", "dve_c")
        s.aft("act", ev_d)
        s.op("act", lambda eng: eng.activation(kt(30), kt(30), AF.Sin))
        s.op("act", lambda eng: eng.activation(kt(29), kt(29), AF.Sin))
        ev_a = s.sig("act", "act_c")
        s.aft("dve", ev_a)
        s.op("dve", lambda eng: eng.tensor_mul(kt(0), kt(31), kt(29)))
        s.op("dve", lambda eng: eng.tensor_mul(kt(6), kt(31), kt(30)))
        s.op("dve", lambda eng: eng.tensor_scalar(kt(29), kt(0), -1.0, None, ALU.add))
        s.op("dve", lambda eng: eng.tensor_mul(kt(30), kt(25), kt(25)))
        s.op("dve", lambda eng: eng.tensor_mul(kt(31), kt(26), kt(26)))
        s.op("dve", lambda eng: eng.tensor_add(kt(30), kt(30), kt(31)))
        s.op("dve", lambda eng: eng.reciprocal(kt(30), kt(30)))
        s.op("dve", lambda eng: eng.tensor_mul(kt(31), kt(29), kt(25)))
        s.op("dve", lambda eng: eng.tensor_mul(kt(24), kt(6), kt(26)))
        s.op("dve", lambda eng: eng.tensor_add(kt(31), kt(31), kt(24)))
        s.op("dve", lambda eng: eng.tensor_mul(kt(31), kt(31), kt(30)))
        s.op("dve", lambda eng: eng.tensor_mul(kt(24), kt(6), kt(25)))
        s.op("dve", lambda eng: eng.tensor_mul(kt(25), kt(29), kt(26)))
        s.op("dve", lambda eng: eng.tensor_sub(kt(24), kt(24), kt(25)))
        s.op("dve", lambda eng: eng.tensor_mul(kt(24), kt(24), kt(30)))
        s.op("dve", lambda eng: eng.tensor_mul(kt(25), kt(27), kt(31)))
        s.op("dve", lambda eng: eng.tensor_mul(kt(26), kt(28), kt(24)))
        s.op("dve", lambda eng: eng.tensor_sub(kt(25), kt(25), kt(26)))
        s.op("dve", lambda eng: eng.tensor_scalar(kt(25), kt(25), 2.0, None, ALU.mult))
        s.op("dve", lambda eng: eng.tensor_mul(kt(26), kt(27), kt(24)))
        s.op("dve", lambda eng: eng.tensor_mul(kt(29), kt(28), kt(31)))
        s.op("dve", lambda eng: eng.tensor_add(kt(26), kt(26), kt(29)))
        s.op("dve", lambda eng: eng.tensor_scalar(kt(26), kt(26), 2.0, None, ALU.mult))
        def csq(sre, sim, dre, dim):
            s.op("dve", lambda eng: eng.tensor_mul(kt(29), kt(sre), kt(sre)))
            s.op("dve", lambda eng: eng.tensor_mul(kt(30), kt(sim), kt(sim)))
            s.op("dve", lambda eng: eng.tensor_sub(kt(dre), kt(29), kt(30)))
            s.op("dve", lambda eng: eng.tensor_mul(kt(29), kt(sre), kt(sim)))
            s.op("dve", lambda eng: eng.tensor_scalar(kt(dim), kt(29), 2.0, None, ALU.mult))
        for j in range(5):
            csq(j, 6 + j, j + 1, 6 + j + 1)
        csq(5, 11, 12, 18)
        for j in range(5):
            csq(12 + j, 18 + j, 12 + j + 1, 18 + j + 1)
        ev_ktab = s.sig("dve", "dve_c")

        ev_ktm = None
        s.aft("dve", ev_pe_c)
        for hp in range(8):
            s.op("dve", lambda eng, hp=hp: eng.tensor_copy(
                R2[0:32, 0:16], kt(25)[:, hp * 16:(hp + 1) * 16]))
            s.op("dve", lambda eng, hp=hp: eng.tensor_copy(
                R4[0:32, 0:16], kt(26)[:, hp * 16:(hp + 1) * 16]))
            s.op("dve", lambda eng: eng.memset(R1[0:32, 0:16], 1.0))
            s.op("dve", lambda eng: eng.memset(R3[0:32, 0:16], 0.0))
            for j in range(6):
                bl = 16 * (1 << j)
                for (are, aim, pre, pim) in ((R2, R4, j, 6 + j), (R1, R3, 12 + j, 18 + j)):
                    s.op("dve", lambda eng, pre=pre, hp=hp: eng.tensor_copy(
                        R0[0:32, 0:16], kt(pre)[:, hp * 16:(hp + 1) * 16]))
                    s.op("dve", lambda eng, pim=pim, hp=hp: eng.tensor_copy(
                        R0[0:32, 512:528], kt(pim)[:, hp * 16:(hp + 1) * 16]))
                    cur = 16
                    while cur < bl:
                        s.op("dve", lambda eng, cur=cur: eng.tensor_copy(
                            R0[0:32, cur:2 * cur], R0[0:32, 0:cur]))
                        s.op("dve", lambda eng, cur=cur: eng.tensor_copy(
                            R0[0:32, 512 + cur:512 + 2 * cur], R0[0:32, 512:512 + cur]))
                        cur *= 2
                    s.op("dve", lambda eng, a=are, bl=bl: eng.tensor_mul(
                        R0[0:32, 1024:1024 + bl], a[0:32, 0:bl], R0[0:32, 0:bl]))
                    s.op("dve", lambda eng, a=aim, bl=bl: eng.tensor_mul(
                        R0[0:32, 1536:1536 + bl], a[0:32, 0:bl], R0[0:32, 512:512 + bl]))
                    s.op("dve", lambda eng, a=are, bl=bl: eng.tensor_sub(
                        a[0:32, bl:2 * bl], R0[0:32, 1024:1024 + bl], R0[0:32, 1536:1536 + bl]))
                    s.op("dve", lambda eng, a=are, bl=bl: eng.tensor_mul(
                        R0[0:32, 1024:1024 + bl], a[0:32, 0:bl], R0[0:32, 512:512 + bl]))
                    s.op("dve", lambda eng, a=aim, bl=bl: eng.tensor_mul(
                        R0[0:32, 1536:1536 + bl], a[0:32, 0:bl], R0[0:32, 0:bl]))
                    s.op("dve", lambda eng, a=aim, bl=bl: eng.tensor_add(
                        a[0:32, bl:2 * bl], R0[0:32, 1024:1024 + bl], R0[0:32, 1536:1536 + bl]))
            s.op("dve", lambda eng: eng.tensor_scalar(
                R4[0:32, 0:1024], R4[0:32, 0:1024], -1.0, None, ALU.mult))
            ev_pass = s.sig("dve", "dve_c")
            if hp == 0:
                emit_attn_a.ev_gte = list(ev_evac)
            for hh in range(16):
                h = hp * 16 + hh
                w = hh % 4
                s.aft("pe", ev_pass, emit_attn_a.ev_gte[w])
                gsl = R2[0:32, 0:1024].rearrange("p (m h) -> p h m", h=16)[:, hh, :]
                gisl = R4[0:32, 0:1024].rearrange("p (m h) -> p h m", h=16)[:, hh, :]
                tsl = R1[0:32, 0:1024].rearrange("p (m h) -> p h m", h=16)[:, hh, :]
                tisl = R3[0:32, 0:1024].rearrange("p (m h) -> p h m", h=16)[:, hh, :]
                s.op("pe", lambda eng, gsl=gsl, tsl=tsl, w=w: eng.matmul(
                    p_mm[w][0:64, 0:64], gsl, tsl, start=True, stop=False))
                s.op("pe", lambda eng, gisl=gisl, tisl=tisl, w=w: eng.matmul(
                    p_mm[w][0:64, 0:64], gisl, tisl, start=False, stop=True))
                ev_gt = s.sig("pe", "pe_c")
                s.aft("dve", ev_gt)
                s.op("dve", lambda eng, h=h, w=w: eng.tensor_copy(
                    wk1[1][0:64, :].rearrange("p (j c) -> p j c", c=128)[:, :, h:h + 1],
                    p_mm[w][0:64, 0:64].rearrange("p (m e) -> p m e", e=2)[:, :, 0:1]))
                s.op("dve", lambda eng, h=h, w=w: eng.tensor_copy(
                    wk1[1][64:128, :].rearrange("p (j c) -> p j c", c=128)[:, :, h:h + 1],
                    p_mm[w][0:64, 0:64].rearrange("p (m e) -> p m e", e=2)[:, :, 1:2]))
                ev_ktm = s.sig("dve", "dve_c")
                emit_attn_a.ev_gte[w] = ev_ktm

        if _APH < 3:
            return ev_ktm
        ev_xst_ = [None, None]
        ev_gth = [None] * 8
        ag1_flat = ag1_out.ap().rearrange("s l p (c h) -> (s l p c) h", h=128)
        for ltg in range(32):
            st = ltg % 2
            for b in range(B):
                s.aft("pool", ev_ag1 if ltg == 0 and b == 0 else None,
                      ev_xst_[st] if b == 0 else None)
                s.op("pool", lambda eng, ltg=ltg, b=b, st=st: eng.indirect_dma_start(
                    xq_ext[5][:, 1024 + st * 512 + b * 128:1024 + st * 512 + (b + 1) * 128],
                    None,
                    ag1_flat,
                    bass.IndirectOffsetOnAxis(ap=sagx1[:, ltg * 4 + b:ltg * 4 + b + 1], axis=0),
                ), isdma=True)
                ev_g = s.sig("pool", f"a_xg{st}", 16)
            s.aft("sp", ev_g)
            ev_xst_[st] = dma("sp", xtm_hbm[ltg],
                              xq_ext[5][:, 1024 + st * 512:1024 + (st + 1) * 512], f"a_xb{st}")
        ev_xb = (s.mark("a_xb0"), s.mark("a_xb1"))

        if _APH < 4:
            return ev_ktm
        NFC = 66

        def xr_ring(ri):
            return xr2[ri][:, :] if ri < 2 else xq_ext[5][:, (ri - 2) * 512:(ri - 1) * 512]
        ev_wf = [None, None]
        ev_xr = [None] * 4
        ev_pmf = [None] * 4
        ev_pbf = [None] * 4
        ev_xstage = [None, None]
        ev_skstage = [None, None]
        ev_xst = [None] * NFC
        ev_skst = [None] * NFC
        for mg in range(17):
            mcs = list(range(4 * mg, min(4 * mg + 4, NFC)))
            wdt = len(mcs) * 128
            for kc in range(32):
                bi = kc % 2
                s.aft("sp", ev_wf[bi])
                evw = dma("sp", wk2[bi][:, 0:wdt], wf_in[kc][:, mg * 512:mg * 512 + wdt],
                          "a_wf_e" if bi == 0 else "a_wf_o")
                ri = kc % 4
                s.aft("pool", ev_xr[ri], *(ev_xb if mg == 0 and kc < 4 else ()))
                evx = dma("pool", xr_ring(ri), xtm_hbm[kc], f"a_xr_{ri}")
                for i, mc in enumerate(mcs):
                    s.aft("pe", evw, evx)
                    if kc == 0:
                        s.aft("pe", ev_pmf[i], ev_pbf[i], ev_ktm if mg == 0 else None,
                              ev_dtileb if mg == 0 and i == 3 else None)
                    s.op("pe", lambda eng, i=i, bi=bi, ri=ri, kc=kc: eng.matmul(
                        p_mm[i][:, :], wk2[bi][:, i * 128:(i + 1) * 128], xr_ring(ri),
                        start=(kc == 0), stop=(kc == 31)))
                    s.op("pe", lambda eng, i=i, bi=bi, ri=ri, kc=kc: eng.matmul(
                        p_big[:, i * 512:i * 512 + 128],
                        wk2[bi][:, i * 128:(i + 1) * 128],
                        wk1[1][:, kc * 128:(kc + 1) * 128],
                        start=(kc == 0), stop=(kc == 31)))
                ev_mmk = s.sig("pe", "pe_c")
                ev_wf[bi] = ev_mmk
                ev_xr[ri] = ev_mmk
            for i, mc in enumerate(mcs):
                st = i % 2
                s.aft("dve", ev_mmk, ev_xstage[st])
                s.op("dve", lambda eng, i=i, st=st: eng.tensor_copy(
                    sacttmp[st][:, :], p_mm[i][:, :]))
                ev_ev = s.sig("dve", "dve_c")
                ev_pmf[i] = ev_ev
                s.aft("sp", ev_ev)
                ev_xstage[st] = dma("sp", x_hbm[mc], sacttmp[st][:, :],
                                    "a_xs_e" if st == 0 else "a_xs_o")
                ev_xst[mc] = ev_xstage[st]
                s.aft("dve", ev_skstage[st])
                s.op("dve", lambda eng, i=i, st=st: eng.tensor_copy(
                    R1[:, 1024 + st * 128:1024 + (st + 1) * 128],
                    p_big[:, i * 512:i * 512 + 128]))
                ev_ev2 = s.sig("dve", "dve_c")
                ev_pbf[i] = ev_ev2
                s.aft("pool", ev_ev2)
                ev_skstage[st] = dma("pool", sk_hbm[mc],
                                     R1[:, 1024 + st * 128:1024 + (st + 1) * 128],
                                     "a_sk_e" if st == 0 else "a_sk_o")
                ev_skst[mc] = ev_skstage[st]

        if _APH < 5:
            return ev_pmf[0]
        ev_ych = [None] * NFC
        ev_px = [None] * 4
        ev_skl_last = None
        ev_yst = [None, None]
        for i in range(33):
            pe_, po_ = (2 * i) % 4, (2 * i + 1) % 4
            sem = "a_px_e" if i % 2 == 0 else "a_px_o"
            s.aft("pool", ev_px[pe_], ev_px[po_], ev_xst[i], ev_xst[33 + i])
            dma("pool", R0[:, pe_ * 512:(pe_ + 1) * 512], x_hbm[i], sem)
            evx2 = dma("pool", R0[:, po_ * 512:(po_ + 1) * 512], x_hbm[33 + i], sem)
            s.aft("pool", ev_skl_last, ev_skst[i], ev_skst[33 + i])
            if i == 0:
                s.aft("pool", s.mark("a_sk_e"), s.mark("a_sk_o"))
            dma("pool", R1[:, 1024:1152], sk_hbm[i], "a_skl_e" if i % 2 == 0 else "a_skl_o")
            evs2 = dma("pool", R1[:, 1152:1280], sk_hbm[33 + i],
                       "a_skl_e" if i % 2 == 0 else "a_skl_o")
            s.aft("dve", evs2)
            if i == 0:
                s.aft("dve", s.mark("a_xs_e"), s.mark("a_xs_o"), s.mark("a_xb0"), s.mark("a_xb1"))
            for q in range(4):
                s.op("dve", lambda eng, q=q: eng.tensor_copy(
                    R1[:, q * 128:(q + 1) * 128], R1[:, 1024:1152]))
                s.op("dve", lambda eng, q=q: eng.tensor_copy(
                    R1[:, 512 + q * 128:512 + (q + 1) * 128], R1[:, 1152:1280]))
            ev_skl_last = s.sig("dve", "dve_c")
            s.aft("dve", evx2, ev_yst[0], ev_yst[1])
            s.op("dve", lambda eng, pe_=pe_: eng.tensor_mul(
                sacttmp[1][:, :], R0[:, pe_ * 512:(pe_ + 1) * 512], R1[:, 0:512]))
            s.op("dve", lambda eng, po_=po_: eng.tensor_mul(
                h_t[:, :], R0[:, po_ * 512:(po_ + 1) * 512], R1[:, 512:1024]))
            s.op("dve", lambda eng: eng.tensor_sub(
                xq_ext[5][:, 1024:1536], sacttmp[1][:, :], h_t[:, :]))
            ev_yre = s.sig("dve", "dve_c")
            s.op("dve", lambda eng, pe_=pe_: eng.tensor_mul(
                sacttmp[1][:, :], R0[:, pe_ * 512:(pe_ + 1) * 512], R1[:, 512:1024]))
            s.op("dve", lambda eng, po_=po_: eng.tensor_mul(
                h_t[:, :], R0[:, po_ * 512:(po_ + 1) * 512], R1[:, 0:512]))
            s.op("dve", lambda eng: eng.tensor_add(
                xq_ext[5][:, 1536:2048], sacttmp[1][:, :], h_t[:, :]))
            ev_yim = s.sig("dve", "dve_c")
            ev_px[pe_] = ev_yim
            ev_px[po_] = ev_yim
            s.aft("sp", ev_yre)
            ev_yst[0] = dma("sp", y_hbm[i], xq_ext[5][:, 1024:1536], "a_ys_e")
            ev_ych[i] = ev_yst[0]
            s.aft("sp", ev_yim)
            ev_yst[1] = dma("sp", y_hbm[33 + i], xq_ext[5][:, 1536:2048], "a_ys_o")
            ev_ych[33 + i] = ev_yst[1]

        if _APH < 6:
            return ev_ych[65]
        def inv_ps(m):
            return p_mm[m][:, :] if m < 4 else p_big[:, (m - 4) * 512:(m - 3) * 512]
        ev_iw = [None, None]
        ev_yl = [None] * 4
        ev_pmfree = [None] * 8
        ev_xt2 = [None, None]
        ev_gst = [None, None]
        for tcg in range(4):
            for fc in range(NFC):
                bi = fc % 2
                s.aft("sp", ev_iw[bi])
                evw = dma("sp", xq2r[bi][:, :], iw_in[tcg][fc], "a_iw_e" if bi == 0 else "a_iw_o")
                ri = fc % 4
                s.aft("pool", ev_yl[ri], ev_ych[fc])
                evy = dma("pool", xq_ext[6][:, ri * 512:(ri + 1) * 512], y_hbm[fc],
                          f"a_yl_{ri}")
                for m in range(8):
                    s.aft("pe", evw, evy)
                    if fc == 0:
                        s.aft("pe", ev_pmfree[m] if tcg > 0 else
                              (ev_pmf[m] if m < 4 else ev_pbf[m - 4]))
                    s.op("pe", lambda eng, m=m, bi=bi, ri=ri, fc=fc: eng.matmul(
                        inv_ps(m), xq2r[bi][:, m * 128:(m + 1) * 128],
                        xq_ext[6][:, ri * 512:(ri + 1) * 512],
                        start=(fc == 0), stop=(fc == NFC - 1)))
                ev_mmk = s.sig("pe", "pe_c")
                ev_iw[bi] = ev_mmk
                ev_yl[ri] = ev_mmk
            ev_invdone = s.mark("pe_c")
            def gslot(m):
                return (R0 if m < 4 else R1)[:, (m % 4) * 512:(m % 4 + 1) * 512]
            ev_t2l = None
            ev_gelv = [None] * 8
            for m in range(8):
                tc = tcg * 8 + m
                st = m % 2
                s.aft("pool", ev_xt2[st])
                evxt = dma("pool", xr2[st][:, :], xtm_hbm[tc],
                           "a_xt_e" if st == 0 else "a_xt_o")
                s.aft("dve", ev_invdone, evxt, ev_dtileb if tcg == 0 and m == 0 else None,
                      ev_gelv[m] if tcg > 0 else None,
                      ev_gelv[m - 1] if m > 0 else None)
                s.op("dve", lambda eng, st=st: eng.tensor_mul(
                    sacttmp[1][:, :], xr2[st][:, :], convtmp[:, 0:512]))
                s.op("dve", lambda eng, m=m: eng.tensor_add(
                    sacttmp[0][:, :], sacttmp[1][:, :], inv_ps(m)))
                ev_t2 = s.sig("dve", "dve_c")
                ev_xt2[st] = ev_t2
                if m < 4:
                    ev_pmfree[m] = ev_t2
                s.aft("act", ev_t2)
                s.op("act", lambda eng, m=m: eng.activation(
                    gslot(m), sacttmp[0][:, :], AF.Gelu))
                ev_gelv[m] = s.sig("act", "act_c")
                ev_t2l = ev_t2
            if _os.environ.get("ATTN_SUB") == "epA":
                for m in range(4, 8):
                    ev_pmfree[m] = ev_t2l
                continue
            _MB = int(_os.environ.get("ATTN_MB", "8"))
            _QB = int(_os.environ.get("ATTN_QB", "4"))
            ev_tq = [[None] * 4 for _ in range(4)]
            ev_geq = [[None] * 4 for _ in range(4)]
            for m in range(8):
                if m >= _MB:
                    ev_pmfree[m] = ev_t2l if m >= 4 else ev_pmfree[m]
                    continue
                tc = tcg * 8 + m
                for q in range(_QB):
                    s.aft("pe", ev_gelv[m], ev_geq[q][0],
                          ev_t2l if m == 0 and q == 0 else None)
                    s.op("pe", lambda eng, m=m, q=q: eng.transpose(
                        p_big[:, q * 512:q * 512 + 128],
                        gslot(m).rearrange("p (q c) -> p q c", c=128)[:, q, :], sidf))
                    ev_tp = s.sig("pe", "pe_c")
                    ev_tq[q][0] = ev_tp
                    s.aft("dve", ev_tp, ev_gst[q % 2])
                    s.op("dve", lambda eng, m=m, q=q: eng.tensor_copy(
                        castbuf[:, (q % 2) * 128:(q % 2 + 1) * 128],
                        p_big[:, q * 512:q * 512 + 128]))
                    ev_ge = s.sig("dve", "dve_c")
                    ev_geq[q][0] = ev_ge
                    if _os.environ.get("ATTN_SUB") == "nost":
                        ev_gst[q % 2] = ev_ge
                        continue
                    s.aft("sp", ev_ge)
                    ev_gst[q % 2] = dma(
                        "sp", ag2_in.ap()[:, q * 4096 + tc * 128:q * 4096 + (tc + 1) * 128],
                        castbuf[:, (q % 2) * 128:(q % 2 + 1) * 128],
                        "a_gs_e" if q % 2 == 0 else "a_gs_o")
            ev_pb = s.mark("dve_c")
            for m in range(4, min(_MB, 8)):
                ev_pmfree[m] = ev_pb
        if _os.environ.get("ATTN_SUB") in ("noep", "noag2"):
            return ev_pmfree[7]
        s.aft("pool", s.mark("a_gs_e"), s.mark("a_gs_o"))
        s.op("pool", lambda eng: eng.collective_compute(
            "AllGather", ALU.bypass, ins=[ag2_in.ap()], outs=[ag2_out.ap()],
            replica_groups=[list(range(8))]), isdma=True)
        ev_ag2 = s.sig("pool", "ccs")

        if _APH < 7:
            return ev_pmfree[7]
        ag2_flat = ag2_out.ap().rearrange("s p (c t) -> (s p c) t", t=512)
        wat_r = wat.rearrange("(k p) o -> k p o", p=128)
        ev_wa = [None, None]
        ev_last = None
        ev_mmw = None
        ev_glu = [None] * 8
        ev_lnl = [None, None]
        for lt in range(4):
            for j in range(8):
                s.aft("pool", ev_ag2 if lt == 0 and j == 0 else None,
                      ev_mmw if j == 0 else None)
                s.op("pool", lambda eng, j=j, lt=lt: eng.indirect_dma_start(
                    wk1[j // 4][:, (j % 4) * 512:(j % 4 + 1) * 512],
                    None,
                    ag2_flat,
                    bass.IndirectOffsetOnAxis(ap=sagx2[:, j * 4 + lt:j * 4 + lt + 1], axis=0),
                ), isdma=True)
                ev_rh = s.sig("pool", "a_rh", 16)
            for wave in range(2):
                for k in range(8):
                    bi = k % 2
                    s.aft("pool", ev_wa[bi])
                    evw = dma("pool", xq2r[bi][:, :], wat_r[k][:, wave * 1024:(wave + 1) * 1024],
                              "a_wa_e" if bi == 0 else "a_wa_o")
                    for oc in range(8):
                        s.aft("pe", evw, ev_rh)
                        if k == 0:
                            s.aft("pe", ev_glu[(wave * 4 + (oc % 4) + 4) % 8])
                        if lt == 0 and wave == 0 and k == 0 and oc == 0:
                            s.aft("pe", s.mark("dve_c"))
                        if oc < 4:
                            s.op("pe", lambda eng, oc=oc, bi=bi, k=k: eng.matmul(
                                p_mm[oc][:, :], xq2r[bi][:, oc * 128:(oc + 1) * 128],
                                wk1[k // 4][:, (k % 4) * 512:(k % 4 + 1) * 512],
                                start=(k == 0), stop=(k == 7)))
                        else:
                            s.op("pe", lambda eng, oc=oc, bi=bi, k=k: eng.matmul(
                                p_big[:, (oc - 4) * 512:(oc - 3) * 512],
                                xq2r[bi][:, oc * 128:(oc + 1) * 128],
                                wk1[k // 4][:, (k % 4) * 512:(k % 4 + 1) * 512],
                                start=(k == 0), stop=(k == 7)))
                    ev_mmw = s.sig("pe", "pe_c")
                    ev_wa[bi] = ev_mmw
                for pr in range(4):
                    m = wave * 4 + pr
                    li = m % 2
                    s.aft("pool", ev_lnl[li])
                    evln = dma("pool", R2[:, li * 512:(li + 1) * 512],
                               ln0_hbm[m][:, lt * 512:(lt + 1) * 512],
                               "a_lnl_e" if li == 0 else "a_lnl_o")
                    s.aft("dve", ev_mmw, evln)
                    s.op("dve", lambda eng, pr=pr, wave=wave: eng.tensor_scalar(
                        sacttmp[0][:, :], p_mm[pr][:, :],
                        swabc(wave * 8 + pr), None, ALU.add))
                    ev_ta = s.sig("dve", "dve_c")
                    s.aft("act", ev_ta)
                    s.op("act", lambda eng, pr=pr, wave=wave: eng.activation(
                        sacttmp[1][:, :], p_big[:, pr * 512:(pr + 1) * 512], AF.Sigmoid,
                        bias=swabc(wave * 8 + 4 + pr)))
                    ev_tg = s.sig("act", "act_c")
                    s.aft("dve", ev_tg)
                    s.op("dve", lambda eng: eng.tensor_mul(
                        h_t[:, :], sacttmp[0][:, :], sacttmp[1][:, :]))
                    s.op("dve", lambda eng, li=li: eng.tensor_add(
                        h_t[:, :], h_t[:, :], R2[:, li * 512:(li + 1) * 512]))
                    s.op("dve", lambda eng, m=m, lt=lt: eng.tensor_add(
                        resid[m][:, lt * 512:(lt + 1) * 512],
                        resid[m][:, lt * 512:(lt + 1) * 512], h_t[:, :]))
                    ev_last = s.sig("dve", "dve_c")
                    ev_glu[m] = ev_last
                    ev_lnl[li] = ev_last
        s.pending["sp"].append(s.mark("pe_c"))
        s.pending["pool"].append(s.mark("pe_c"))
        return ev_last

    ev = ev_x
    mamba_n = [0]
    first_ffn = [True]
    for sl in sublayers:
        if sl == 1:
            ev = emit_attn_a(ev)
        elif sl in (2, 4):
            ev = emit_ffn(ev, first_ffn[0])
            first_ffn[0] = False
        elif sl == 3:
            ev = emit_mamba(ev, mamba_n[0])
            mamba_n[0] += 1

    y_t = y_out.rearrange("(c p) l -> c p l", p=128)
    for c in range(DC):
        s.aft("sp", ev)
        dma("sp", y_t[c], resid[c][:, :], "st_y")
    ev_done = s.mark("st_y")
    s.aft("pool", ev_done)
    s.op("pool", lambda eng: eng.memset(thin[:, 15:16], 0))

    for extra in ("rsem", "lsem"):
        s.counts.setdefault(extra, 0)
    sems = {n: ctx.enter_context(nc.semaphore(n)) for n in s.counts}
    semd.update(sems)
    with nc.Block() as block:

        @block.tensor
        def _(eng):
            s.replay("pe", eng, sems)

        @block.scalar
        def _(eng):
            s.replay("act", eng, sems)

        @block.vector
        def _(eng):
            s.replay("dve", eng, sems)

        @block.gpsimd
        def _(eng):
            s.replay("pool", eng, sems)

        @block.sync
        def _(eng):
            s.replay("sp", eng, sems)

    ctx.close()
    return nc


_NC_CACHE = {}
_ATTN_CONSTS = {}


def _attn_consts():
    if _ATTN_CONSTS:
        return _ATTN_CONSTS
    import ml_dtypes
    NFFT, LX, NBP = 8192, 4096, 4224
    t = np.arange(LX, dtype=np.int64)
    f = np.arange(NBP, dtype=np.int64)
    r = (t[:, None] * f[None, :]) % NFFT
    costab = np.cos(2 * np.pi * np.arange(NFFT) / NFFT)
    sintab = np.sin(2 * np.pi * np.arange(NFFT) / NFFT)
    cos_m = costab[r]
    sin_m = sintab[r]
    valid = (f < 4097).astype(np.float64)[None, :]
    wf = np.concatenate([cos_m * valid, -sin_m * valid], axis=1)
    w_f = np.where((f == 0) | (f == 4096), 1.0, 2.0) * valid[0]
    iw = np.concatenate([(w_f[:, None] * cos_m.T) / NFFT,
                         -(w_f[:, None] * sin_m.T) / NFFT], axis=0)
    del r, cos_m, sin_m
    _ATTN_CONSTS["wf"] = np.ascontiguousarray(
        wf.reshape(32, 128, 8448).astype(ml_dtypes.bfloat16))
    _ATTN_CONSTS["iw"] = np.ascontiguousarray(
        iw.reshape(66, 128, 4, 8, 128).transpose(2, 0, 1, 3, 4)
        .reshape(4, 66, 128, 1024).astype(ml_dtypes.bfloat16))
    _ATTN_CONSTS["ident"] = np.eye(128, dtype=ml_dtypes.bfloat16)
    _ATTN_CONSTS["identf"] = np.eye(128, dtype=np.float32)
    return _ATTN_CONSTS


def _get_nc(sublayers):
    key = tuple(sublayers)
    if key not in _NC_CACHE:
        _NC_CACHE[key] = build_nc(sublayers)
    return _NC_CACHE[key]


def kernel(**inputs):
    x = np.ascontiguousarray(inputs["L0_attn_a_x"])
    w1 = np.ascontiguousarray(inputs["L0_ffn_w1"])
    w2 = np.ascontiguousarray(inputs["L0_ffn_w2"])
    cw = np.ascontiguousarray(inputs["L0_ffn_conv_w"])
    cb = np.ascontiguousarray(inputs["L0_ffn_conv_b"])

    w1t = np.ascontiguousarray(w1.T)
    w2t = np.ascontiguousarray(w2.T)
    cw_t = np.ascontiguousarray(cw[:, 0, :].T)
    cb_t = np.ascontiguousarray(cb[None, :])

    ipw = np.ascontiguousarray(inputs["L0_attn_b_in_proj_w"])
    xpw_full = np.ascontiguousarray(inputs["L0_attn_b_x_proj_w"])
    dtpw_full = np.ascontiguousarray(inputs["L0_attn_b_dt_proj_w"])
    dtb_full = np.ascontiguousarray(inputs["L0_attn_b_dt_proj_b"])
    alog_full = np.ascontiguousarray(inputs["L0_attn_b_A_log"])
    mD_full = np.ascontiguousarray(inputs["L0_attn_b_D"])
    opw_full = np.ascontiguousarray(inputs["L0_attn_b_out_proj_w"])
    mcw_full = np.ascontiguousarray(inputs["L0_attn_b_conv_w"])
    mcb_full = np.ascontiguousarray(inputs["L0_attn_b_conv_b"])
    selnf = np.zeros((16, 2048), dtype=np.float32)
    for n_ in range(16):
        selnf[n_, n_ * 128:(n_ + 1) * 128] = 1.0

    ac = _attn_consts() if 1 in SUBLAYERS else {}
    if 1 in SUBLAYERS:
        a_cre = np.ascontiguousarray(inputs["L0_attn_a_C_re"])
        a_cim = np.ascontiguousarray(inputs["L0_attn_a_C_im"])
        a_lar = np.ascontiguousarray(inputs["L0_attn_a_log_A_re"])
        a_aim = np.ascontiguousarray(inputs["L0_attn_a_A_im"])
        a_ldt = np.ascontiguousarray(inputs["L0_attn_a_log_dt"])
        a_D = np.ascontiguousarray(inputs["L0_attn_a_D"])
        a_W = np.ascontiguousarray(inputs["L0_attn_a_W"])
        a_b = np.ascontiguousarray(inputs["L0_attn_a_b"])
        perm = np.r_[0:512, 1024:1536, 512:1024, 1536:2048]
        wat_h = np.ascontiguousarray(a_W.T[:, perm])
        wab_h = np.ascontiguousarray(a_b[perm].reshape(16, 128).T)

    nc = _get_nc(SUBLAYERS)
    in_maps = []
    for c in range(N_CORES):
        b, hf = c // 2, c % 2
        hd = hf
        sl_ = slice(hd * 1024, (hd + 1) * 1024)
        alog_c = alog_full[sl_].reshape(8, 128, 16).transpose(1, 0, 2).reshape(128, 128)
        mcw_c = mcw_full[:, 0, sl_].reshape(4, 8, 128).transpose(2, 1, 0).reshape(128, 32)
        mcb_c = mcb_full[sl_].reshape(8, 128).T.copy()
        mdtb_c = dtb_full[sl_].reshape(8, 128).T.copy()
        mdc_c = mD_full[sl_].reshape(8, 128).T.copy()
        x_fm = np.ascontiguousarray(x[b, hf * LL:(hf + 1) * LL, :].T)
        maskPN = np.zeros((1, 16), dtype=np.float32)
        if hf == 1:
            maskPN[0, c - 1] = 1.0
        if hf == 0:
            maskPN[0, 8 + c + 1] = 1.0
        lnones = np.zeros((128, 3), dtype=np.float32)
        lnones[:, 0] = 1.0
        lnones[:, 2] = 1.0
        import ml_dtypes
        in_maps.append({
            "x_fm": x_fm,
            "w1t": w1t,
            "w2t": w2t,
            "ffn_cw": cw_t,
            "ffn_cb": cb_t,
            "maskPN": maskPN,
            "onesrow": np.ones((1, 128), dtype=np.float32),
            "onescol": np.ones((128, 1), dtype=np.float32),
            "lnones": lnones.astype(ml_dtypes.bfloat16),
            "inpw_x": np.ascontiguousarray(ipw[:, hd * 1024:(hd + 1) * 1024]),
            "inpw_z": np.ascontiguousarray(ipw[:, 2048 + hd * 1024:2048 + (hd + 1) * 1024]),
            "xpw": np.ascontiguousarray(xpw_full[sl_, :]),
            "dtpw": np.ascontiguousarray(dtpw_full[:, sl_]),
            "opw": opw_full,
            "alog": np.ascontiguousarray(alog_c.astype(np.float32)),
            "mcw": np.ascontiguousarray(mcw_c.astype(np.float32)),
            "mcb": np.ascontiguousarray(mcb_c.astype(np.float32)),
            "mdtb": np.ascontiguousarray(mdtb_c.astype(np.float32)),
            "mdc": np.ascontiguousarray(mdc_c.astype(np.float32)),
            "selnf": selnf,
            "halfsel": np.asarray([[1.0 if sp_ // 4 == hf else 0.0 for sp_ in range(8)]],
                                  dtype=np.float32),
        })
        if 1 in SUBLAYERS:
            hs = slice(c * 128, (c + 1) * 128)
            ltg_ = np.arange(32)
            bb_ = np.arange(4)
            slot_ = bb_[None, :] * 2 + (ltg_[:, None] >= 16)
            lt_ = ltg_ % 16
            rows = slot_ * 2048 + lt_[:, None] * 128
            agidx1 = (np.arange(128)[:, None] + rows.reshape(1, 128)) * 8 + c
            j_ = np.arange(8)
            lt4 = np.arange(4)
            agidx2 = ((np.arange(128)[:, None, None] + j_[None, :, None] * 128) * 32
                      + (b * 8 + hf * 4 + lt4)[None, None, :]).reshape(128, 32)
            in_maps[-1].update({
                "ident": ac["ident"],
                "identf": ac["identf"],
                "wf": ac["wf"],
                "iw": ac["iw"],
                "agidx1": np.ascontiguousarray(agidx1.astype(np.int32)),
                "agidx2": np.ascontiguousarray(agidx2.astype(np.int32)),
                "care_t": np.ascontiguousarray(a_cre[hs].T),
                "cimt": np.ascontiguousarray(a_cim[hs].T),
                "lare_t": np.ascontiguousarray(a_lar[hs].T),
                "aimt": np.ascontiguousarray(a_aim[hs].T),
                "ldt_r": np.ascontiguousarray(a_ldt[hs][None, :]),
                "dtile": np.ascontiguousarray(
                    a_D[c * 128 + (np.arange(512) % 128)][None, :].astype(np.float32)),
                "wat": wat_h,
                "wab": wab_h,
            })
    res = run_bass_kernel_spmd(nc, in_maps, core_ids=list(range(N_CORES)))
    out = np.empty((B, L, D), dtype=np.float32)
    for c in range(N_CORES):
        b, hf = c // 2, c % 2
        out[b, hf * LL:(hf + 1) * LL, :] = res.results[c]["y_fm"].T
    return out
